# version 13
# baseline (speedup 1.0000x reference)
import contextlib
import numpy as np
import ml_dtypes
import concourse.bass as bass
import concourse.mybir as mybir
from concourse.ap import AP
from concourse.bass_utils import run_bass_kernel_spmd

F32 = mybir.dt.float32
BF16 = mybir.dt.bfloat16
I32 = mybir.dt.int32
AF = mybir.ActivationFunctionType
ALU = mybir.AluOpType

NCORE = 8
R = 2048
D = 4096
DFF = 11008
NT = D // 128
NFT = DFF // 128
PAN = 512
NPAN = R // PAN
EPS = 1e-5

_CONST_CACHE = {}


def _host_constants():
    if _CONST_CACHE:
        return _CONST_CACHE
    L = D
    lvec = (np.arange(L, dtype=np.float32) / np.float32(L)).astype(np.float32)
    arg = (np.complex64(-2j * np.pi) * lvec.astype(np.complex64)).astype(np.complex64)
    Omega = np.exp(arg).astype(np.complex64)
    cc = (np.complex64(2) / (np.complex64(1) + Omega)).astype(np.complex64)
    c2 = (np.complex64(2) * ((np.complex64(1) - Omega) / (np.complex64(1) + Omega))).astype(np.complex64)
    f32 = np.float32
    cb = np.empty((64, 2 * L), f32)
    cb[:, :L] = c2.real.astype(f32)[None, :]
    cb[:, L:] = c2.imag.astype(f32)[None, :]
    W64 = np.exp(2j * np.pi * np.outer(np.arange(64), np.arange(64)) / 64.0)
    TW = np.exp(2j * np.pi * np.outer(np.arange(64), np.arange(64)) / 4096.0) / 4096.0
    tab = np.zeros((64, 448), f32)
    tab[:, 128:192] = W64.real.astype(f32)
    tab[:, 192:256] = W64.imag.astype(f32)
    tab[:, 256:320] = (-W64.imag).astype(f32)
    tab[:, 320:384] = TW.real.astype(f32)
    tab[:, 384:448] = TW.imag.astype(f32)
    m128 = np.empty((128, 192), f32)
    m128[:, 0:32] = cc.real.astype(f32).reshape(32, 128).T
    m128[:, 32:64] = cc.imag.astype(f32).reshape(32, 128).T
    m128[:, 64:] = np.eye(128, dtype=f32)
    jx = np.zeros((128, 128), ml_dtypes.bfloat16)
    jx[np.arange(128), 127 - np.arange(128)] = 1
    _CONST_CACHE.update(cb=cb, tab=tab, m128=m128, jx=jx)
    return _CONST_CACHE


def build_program(phases="KABC", dbg=False):
    nc = bass.Bass("TRN2", target_bir_lowering=False)

    uT = nc.dram_tensor("uT", [D, R], F32, kind="ExternalInput").ap()
    gwl = nc.dram_tensor("gwl", [NFT, 128, D], I32, kind="ExternalInput").ap()
    uwl = nc.dram_tensor("uwl", [NFT, 128, D], I32, kind="ExternalInput").ap()
    dwT = nc.dram_tensor("dwT", [DFF, D], I32, kind="ExternalInput").ap()
    gsp = nc.dram_tensor("gsp", [128, NFT], F32, kind="ExternalInput").ap()
    usp = nc.dram_tensor("usp", [128, NFT], F32, kind="ExternalInput").ap()
    dsp = nc.dram_tensor("dsp", [128, NT], F32, kind="ExternalInput").ap()
    small64_i = nc.dram_tensor("small64", [64, 9], F32, kind="ExternalInput").ap()
    tab64_i = nc.dram_tensor("tab64", [64, 448], F32, kind="ExternalInput").ap()
    cb_i = nc.dram_tensor("cb", [64, 2 * D], F32, kind="ExternalInput").ap()
    m128_i = nc.dram_tensor("m128", [128, 192], F32, kind="ExternalInput").ap()
    jx_i = nc.dram_tensor("jx", [128, 128], BF16, kind="ExternalInput").ap()
    sdv_i = nc.dram_tensor("sdv", [1, 2], F32, kind="ExternalInput").ap()

    outT = nc.dram_tensor("outT", [D, R], F32, kind="ExternalOutput").ap()
    scr_kind = "ExternalOutput" if dbg else "Internal"
    kpad = nc.dram_tensor("kpad", [4352], F32, kind=scr_kind).ap()
    arscr = nc.dram_tensor("arscr", [2, D], F32, kind=scr_kind).ap()
    x1scr = nc.dram_tensor("x1scr", [D, R], F32, kind=scr_kind).ap()
    x1lscr = nc.dram_tensor("x1lscr", [D, R], BF16, kind=scr_kind).ap()
    hscr = nc.dram_tensor("hscr", [DFF, R], BF16, kind=scr_kind).ap()
    dwbf = nc.dram_tensor("dwbf", [DFF, D], BF16, kind=scr_kind).ap()

    st = contextlib.ExitStack()

    sem_names = ["sY", "sG", "sT", "sV", "sA", "sP", "sX1", "sHp0", "sHp1", "sHp2",
                 "sH0", "sH1", "sDw0", "sDw1", "sWi", "sXl0", "sXl1", "sUp0", "sUp1", "sX30", "sX31", "sWd0", "sWd1", "sXt0", "sXt1", "sOt0", "sOt1"]
    sems = {n: st.enter_context(nc.semaphore(n)) for n in sem_names}
    cnt = {n: 0 for n in sem_names}
    waited = {e: {} for e in ["T", "V", "A", "G", "Y"]}

    def I(instr, sem, amt=1):
        instr.then_inc(sems[sem], amt)
        cnt[sem] += amt
        return cnt[sem]

    def W(eng, ename, *pairs):
        for sem, val in pairs:
            if val is None:
                val = cnt[sem]
            if val > waited[ename].get(sem, -1):
                eng.wait_ge(sems[sem], val)
                waited[ename][sem] = val

    ones = st.enter_context(nc.sbuf_tensor("ones", [128, 1], F32))
    onesbf = st.enter_context(nc.sbuf_tensor("onesbf", [128, 1], BF16))
    dcol = st.enter_context(nc.sbuf_tensor("dcol", [128, 1], F32))

    mt_stack = contextlib.ExitStack()
    Mtiles = mt_stack.enter_context(nc.sbuf_tensor("Mtiles", [128, 32 * 128], BF16))

    with st, nc.Block() as block:
        def T(fn): block.tensor(fn)
        def V(fn): block.vector(fn)
        def A(fn): block.scalar(fn)
        def G(fn): block.gpsimd(fn)
        def Y(fn): block.sync(fn)

        kst = contextlib.ExitStack()
        ksb = lambda n, sh, dt: kst.enter_context(nc.sbuf_tensor(n, sh, dt))
        kps = lambda n, sh, dt: kst.enter_context(nc.psum_tensor(n, sh, dt))
        cb = ksb("cbs", [64, 2 * D], F32)
        small64 = ksb("small64s", [64, 9], F32)
        tab64 = ksb("tab64s", [64, 448], F32)
        m128 = ksb("m128s", [128, 192], F32)
        jx = ksb("jxs", [128, 128], BF16)
        sdv = ksb("sdvs", [1, 2], F32)
        grgi = ksb("grgi", [64, 2 * D], F32)
        msq = ksb("msq", [64, D], F32)
        tk1 = ksb("tk1", [64, D], F32)
        rr = ksb("rr", [64, D], F32)
        qp = ksb("qp", [64, D], F32)
        bri = ksb("bri", [64, 2], F32)
        v8 = ksb("v8", [64, 8], F32)
        vt1 = ksb("vt1", [64, 2], F32)
        sA_sb = ksb("sA_sb", [128, 512], F32)
        kk = ksb("kk", [128, 16 * 32], F32)
        invs = ksb("invs", [1, 2], F32)
        inv64 = ksb("inv64", [64, 1], F32)
        arT = ksb("arT", [32, 128], F32)
        ARs = ksb("ARs", [64, 128], F32)
        A1s = ksb("A1s", [64, 128], F32)
        ksmall = ksb("ksmall", [64, 64], F32)
        zrow = ksb("zrow", [1, 256], F32)
        hb2 = [ksb(f"hb2_{i}", [128, 128], BF16) for i in range(2)]
        psA = kps("psA", [128, 512], F32)
        psF1 = kps("psF1", [64, 64], F32)
        psF2 = kps("psF2", [64, 64], F32)
        psK = kps("psK", [64, 64], F32)
        psM = kps("psM", [128, 128], F32)
        psB64 = kps("psB64", [128, 2], F32)

        def _(y):
            I(y.dma_start(cb[:], cb_i), "sY", 16)
            I(y.dma_start(small64[:], small64_i), "sY", 16)
            I(y.dma_start(tab64[:], tab64_i), "sY", 16)
            I(y.dma_start(m128[:], m128_i), "sY", 16)
            I(y.dma_start(jx[:], jx_i), "sY", 16)
            I(y.dma_start(sdv[:], sdv_i), "sY", 16)
        Y(_)
        k_loads = cnt["sY"]

        def _(v):
            I(v.memset(ones[:], 1.0), "sV")
            I(v.memset(onesbf[:], 1.0), "sV")
            I(v.memset(zrow[:], 0.0), "sV")
        V(_)
        m_ones = cnt["sV"]

        def _(s):
            W(s, "A", ("sY", k_loads))
            I(s.activation(invs[0:1, 0:1], sdv[0:1, 1:2], AF.Exp), "sA")
        A(_)
        m_step = cnt["sA"]

        def _(v):
            W(v, "V", ("sA", m_step))
            I(v.reciprocal(invs[0:1, 1:2], invs[0:1, 0:1]), "sV")
        V(_)
        m_invs = cnt["sV"]

        def _(t):
            W(t, "T", ("sV", m_invs))
            I(t.matmul(psB64[0:64, 0:1], ones[0:1, 0:1].to_broadcast([1, 64]), invs[0:1, 1:2],
                       start=True, stop=True), "sT")
            I(t.matmul(psB64[0:128, 1:2], ones[0:1, 0:1].to_broadcast([1, 128]), sdv[0:1, 0:1],
                       start=True, stop=True), "sT")
        T(_)
        m_bc1 = cnt["sT"]

        def _(v):
            W(v, "V", ("sT", m_bc1))
            I(v.tensor_copy(inv64[:], psB64[0:64, 0:1]), "sV")
            I(v.tensor_copy(dcol[:], psB64[0:128, 1:2]), "sV")
        V(_)
        m_bc2 = cnt["sV"]

        def _(v):
            W(v, "V", ("sV", m_bc2))
            I(v.tensor_scalar(grgi[:, :], cb[:, :], inv64[:], None, ALU.mult), "sV")
            W(v, "V", ("sV", None))
            I(v.tensor_scalar(grgi[:, 0:D], grgi[:, 0:D], small64[:, 0:1], None, ALU.subtract), "sV")
            I(v.tensor_scalar(grgi[:, D:], grgi[:, D:], small64[:, 1:2], None, ALU.subtract), "sV")
            W(v, "V", ("sV", None))
            I(v.tensor_tensor(msq[:], grgi[:, 0:D], grgi[:, 0:D], ALU.mult), "sV")
            I(v.tensor_tensor(tk1[:], grgi[:, D:], grgi[:, D:], ALU.mult), "sV")
            W(v, "V", ("sV", None))
            I(v.tensor_tensor(msq[:], msq[:], tk1[:], ALU.add), "sV")
            W(v, "V", ("sV", None))
            I(v.reciprocal(tk1[:], msq[:]), "sV")
            W(v, "V", ("sV", None))
            I(v.tensor_tensor(rr[:], grgi[:, 0:D], tk1[:], ALU.mult), "sV")
            I(v.tensor_tensor(qp[:], grgi[:, D:], tk1[:], ALU.mult), "sV")
        V(_)
        m_R = cnt["sV"]

        def _(t):
            W(t, "T", ("sY", k_loads))
            I(t.matmul(psF1[0:64, 0:1], tab64[:, 0:64], small64[:, 8:9], start=True, stop=True), "sT")
            I(t.matmul(psF1[0:64, 1:2], tab64[:, 64:128], small64[:, 8:9], start=True, stop=True), "sT")
        T(_)
        m_B = cnt["sT"]

        def _(v):
            W(v, "V", ("sT", m_B))
            I(v.tensor_copy(bri[:], psF1[0:64, 0:2]), "sV")
            W(v, "V", ("sV", None))
            def conj_mul(acr, aci, bcr, bci, j):
                I(v.tensor_tensor(vt1[:, 0:1], acr, bcr, ALU.mult), "sV")
                I(v.tensor_tensor(vt1[:, 1:2], aci, bci, ALU.mult), "sV")
                W(v, "V", ("sV", None))
                I(v.tensor_tensor(v8[:, j:j + 1], vt1[:, 0:1], vt1[:, 1:2], ALU.add), "sV")
                W(v, "V", ("sV", None))
                I(v.tensor_tensor(vt1[:, 0:1], acr, bci, ALU.mult), "sV")
                I(v.tensor_tensor(vt1[:, 1:2], aci, bcr, ALU.mult), "sV")
                W(v, "V", ("sV", None))
                I(v.tensor_tensor(v8[:, j + 1:j + 2], vt1[:, 0:1], vt1[:, 1:2], ALU.subtract), "sV")
                W(v, "V", ("sV", None))
            ctr, cti = small64[:, 6:7], small64[:, 7:8]
            pr_, pi_ = small64[:, 2:3], small64[:, 3:4]
            qr_, qi_ = small64[:, 4:5], small64[:, 5:6]
            br_, bi_ = bri[:, 0:1], bri[:, 1:2]
            conj_mul(ctr, cti, br_, bi_, 0)
            conj_mul(ctr, cti, pr_, pi_, 2)
            conj_mul(qr_, qi_, br_, bi_, 4)
            conj_mul(qr_, qi_, pr_, pi_, 6)
        V(_)
        m_v8 = cnt["sV"]

        def _(t):
            W(t, "T", ("sV", max(m_R, m_v8)))
            mm = None
            for c in range(32):
                t.matmul(psA[:, 16 * c:16 * c + 8], rr[:, 128 * c:128 * c + 128], v8[:],
                         start=True, stop=True)
                mm = t.matmul(psA[:, 16 * c + 8:16 * c + 16], qp[:, 128 * c:128 * c + 128], v8[:],
                              start=True, stop=True)
            I(mm, "sT")
        T(_)
        m_cauchy = cnt["sT"]

        def _(v):
            W(v, "V", ("sT", m_cauchy))
            W(v, "V", ("sV", None))
            I(v.tensor_copy(sA_sb[:], psA[:]), "sV")
            W(v, "V", ("sV", None))
            ko = lambda j: sA_sb[:, j:512:16]
            kkv = lambda i: kk[:, 32 * i:32 * i + 32]
            for idx, (jp, jq) in enumerate([(0, 9), (2, 11), (4, 13), (6, 15)]):
                W(v, "V", ("sV", None))
                I(v.tensor_tensor(kkv(2 * idx), ko(jp), ko(jq), ALU.add), "sV")
                W(v, "V", ("sV", None))
                I(v.tensor_tensor(kkv(2 * idx + 1), ko(jp + 1), ko(jq - 1), ALU.subtract), "sV")
            W(v, "V", ("sV", None))
            W(v, "V", ("sV", None))
            I(v.tensor_scalar(kkv(8), kkv(6), 1.0, None, ALU.add), "sV")
            W(v, "V", ("sV", None))
            W(v, "V", ("sV", None))
            I(v.tensor_tensor(kkv(9), kkv(8), kkv(8), ALU.mult), "sV")
            W(v, "V", ("sV", None))
            I(v.tensor_tensor(kkv(12), kkv(7), kkv(7), ALU.mult), "sV")
            W(v, "V", ("sV", None))
            W(v, "V", ("sV", None))
            I(v.tensor_tensor(kkv(9), kkv(9), kkv(12), ALU.add), "sV")
            W(v, "V", ("sV", None))
            W(v, "V", ("sV", None))
            I(v.reciprocal(kkv(9), kkv(9)), "sV")
            W(v, "V", ("sV", None))
            W(v, "V", ("sV", None))
            I(v.tensor_tensor(kkv(12), kkv(4), kkv(8), ALU.mult), "sV")
            W(v, "V", ("sV", None))
            I(v.tensor_tensor(kkv(13), kkv(5), kkv(7), ALU.mult), "sV")
            W(v, "V", ("sV", None))
            W(v, "V", ("sV", None))
            I(v.tensor_tensor(kkv(10), kkv(12), kkv(13), ALU.add), "sV")
            W(v, "V", ("sV", None))
            I(v.tensor_tensor(kkv(12), kkv(5), kkv(8), ALU.mult), "sV")
            W(v, "V", ("sV", None))
            I(v.tensor_tensor(kkv(13), kkv(4), kkv(7), ALU.mult), "sV")
            W(v, "V", ("sV", None))
            W(v, "V", ("sV", None))
            I(v.tensor_tensor(kkv(11), kkv(12), kkv(13), ALU.subtract), "sV")
            W(v, "V", ("sV", None))
            W(v, "V", ("sV", None))
            I(v.tensor_tensor(kkv(10), kkv(10), kkv(9), ALU.mult), "sV")
            W(v, "V", ("sV", None))
            I(v.tensor_tensor(kkv(11), kkv(11), kkv(9), ALU.mult), "sV")
            W(v, "V", ("sV", None))
            W(v, "V", ("sV", None))
            I(v.tensor_tensor(kkv(12), kkv(2), kkv(10), ALU.mult), "sV")
            W(v, "V", ("sV", None))
            I(v.tensor_tensor(kkv(13), kkv(3), kkv(11), ALU.mult), "sV")
            W(v, "V", ("sV", None))
            W(v, "V", ("sV", None))
            I(v.tensor_tensor(kkv(12), kkv(12), kkv(13), ALU.subtract), "sV")
            W(v, "V", ("sV", None))
            W(v, "V", ("sV", None))
            I(v.tensor_tensor(kkv(14), kkv(0), kkv(12), ALU.subtract), "sV")
            W(v, "V", ("sV", None))
            I(v.tensor_tensor(kkv(12), kkv(2), kkv(11), ALU.mult), "sV")
            W(v, "V", ("sV", None))
            I(v.tensor_tensor(kkv(13), kkv(3), kkv(10), ALU.mult), "sV")
            W(v, "V", ("sV", None))
            W(v, "V", ("sV", None))
            I(v.tensor_tensor(kkv(12), kkv(12), kkv(13), ALU.add), "sV")
            W(v, "V", ("sV", None))
            W(v, "V", ("sV", None))
            I(v.tensor_tensor(kkv(15), kkv(1), kkv(12), ALU.subtract), "sV")
            W(v, "V", ("sV", None))
            W(v, "V", ("sV", None))
            I(v.tensor_tensor(kkv(0), m128[:, 0:32], kkv(14), ALU.mult), "sV")
            W(v, "V", ("sV", None))
            I(v.tensor_tensor(kkv(1), m128[:, 32:64], kkv(15), ALU.mult), "sV")
            W(v, "V", ("sV", None))
            I(v.tensor_tensor(kkv(2), m128[:, 0:32], kkv(15), ALU.mult), "sV")
            W(v, "V", ("sV", None))
            I(v.tensor_tensor(kkv(3), m128[:, 32:64], kkv(14), ALU.mult), "sV")
            W(v, "V", ("sV", None))
            W(v, "V", ("sV", None))
            I(v.tensor_tensor(kkv(4), kkv(0), kkv(1), ALU.subtract), "sV")
            W(v, "V", ("sV", None))
            I(v.tensor_tensor(kkv(5), kkv(2), kkv(3), ALU.add), "sV")
        V(_)
        m_ar = cnt["sV"]

        m_psM_free = 0
        for half in range(2):
            def _(t, half=half):
                W(t, "T", ("sV", m_ar))
                if half == 1:
                    W(t, "T", ("sV", m_psM_free))
                nonlocal_box[0] = I(t.transpose(psM[0:32, 0:128],
                                                kk[:, 32 * (4 + half):32 * (4 + half) + 32],
                                                m128[:, 64:192]), "sT")
            nonlocal_box = [0]
            T(_)
            m_tr = nonlocal_box[0]

            def _(v, half=half, m_tr=m_tr):
                W(v, "V", ("sT", m_tr))
                if half == 1:
                    W(v, "V", ("sY", None))
                I(v.tensor_copy(arT[:], psM[0:32, 0:128]), "sV")
            V(_)
            m_psM_free = cnt["sV"]
            m_arT = cnt["sV"]

            def _(y, half=half, m_arT=m_arT):
                W(y, "Y", ("sV", m_arT))
                I(y.dma_start(AP(arscr.tensor, half * D, [[128, 32], [1, 128]]), arT[:]), "sY", 16)
            Y(_)
            m_arst = cnt["sY"]

            def _(y, half=half, m_arst=m_arst):
                W(y, "Y", ("sY", m_arst))
                src = AP(arscr.tensor, half * D, [[64, 64], [1, 64]])
                I(y.dma_start(ARs[:, 64 * half:64 * half + 64], src), "sY", 16)
            Y(_)
        m_AR = cnt["sY"]

        w64r = tab64[:, 128:192]; w64i = tab64[:, 192:256]; w64in = tab64[:, 256:320]
        twr = tab64[:, 320:384]; twi = tab64[:, 384:448]

        def _(t):
            W(t, "T", ("sY", m_AR), ("sV", m_v8))
            t.matmul(psF1[:, :], ARs[:, 0:64], w64r, start=True, stop=False)
            I(t.matmul(psF1[:, :], ARs[:, 64:128], w64in, start=False, stop=True), "sT")
            t.matmul(psF2[:, :], ARs[:, 0:64], w64i, start=True, stop=False)
            I(t.matmul(psF2[:, :], ARs[:, 64:128], w64r, start=False, stop=True), "sT")
        T(_)
        m_st1 = cnt["sT"]

        def _(v):
            W(v, "V", ("sT", m_st1))
            W(v, "V", ("sV", None))
            I(v.tensor_copy(ksmall[:], psF1[:]), "sV")
            W(v, "V", ("sV", None))
            I(v.tensor_copy(ARs[:, 0:64], psF2[:]), "sV")
            W(v, "V", ("sV", None))
            W(v, "V", ("sV", None))
            I(v.tensor_tensor(A1s[:, 0:64], ksmall[:], twr, ALU.mult), "sV")
            W(v, "V", ("sV", None))
            I(v.tensor_tensor(A1s[:, 64:128], ksmall[:], twi, ALU.mult), "sV")
            W(v, "V", ("sV", None))
            I(v.tensor_tensor(ARs[:, 64:128], ARs[:, 0:64], twi, ALU.mult), "sV")
            W(v, "V", ("sV", None))
            W(v, "V", ("sV", None))
            I(v.tensor_tensor(A1s[:, 0:64], A1s[:, 0:64], ARs[:, 64:128], ALU.subtract), "sV")
            W(v, "V", ("sV", None))
            I(v.tensor_tensor(ARs[:, 64:128], ARs[:, 0:64], twr, ALU.mult), "sV")
            W(v, "V", ("sV", None))
            W(v, "V", ("sV", None))
            I(v.tensor_tensor(A1s[:, 64:128], A1s[:, 64:128], ARs[:, 64:128], ALU.add), "sV")
        V(_)
        m_tw = cnt["sV"]

        def _(t):
            W(t, "T", ("sV", m_tw))
            t.matmul(psK[:, :], w64r, A1s[:, 0:64], start=True, stop=False)
            I(t.matmul(psK[:, :], w64in, A1s[:, 64:128], start=False, stop=True), "sT")
        T(_)
        m_st2 = cnt["sT"]

        def _(v):
            W(v, "V", ("sT", m_st2))
            I(v.tensor_copy(ksmall[:], psK[:]), "sV")
        V(_)
        m_ksb = cnt["sV"]

        def _(y):
            W(y, "Y", ("sV", max(m_ones, m_ksb)))
            I(y.dma_start(AP(kpad.tensor, 0, [[127, 1], [1, 127]]), zrow[0:1, 0:127]), "sY", 16)
            I(y.dma_start(AP(kpad.tensor, 4223, [[129, 1], [1, 129]]), zrow[0:1, 0:129]), "sY", 16)
            I(y.dma_start(AP(kpad.tensor, 127, [[64, 64], [1, 64]]), ksmall[:]), "sY", 16)
        Y(_)
        m_kpad = cnt["sY"]

        m_hload = [0] * 32
        m_mdone = [0] * 32
        m_mevac = [0] * 32
        for Dd in range(32):
            def _(g, Dd=Dd):
                W(g, "G", ("sY", m_kpad))
                if Dd >= 2:
                    W(g, "G", ("sT", m_mdone[Dd - 2]))
                src = AP(kpad.tensor, 128 * Dd, [[1, 128], [1, 128]])
                m_hload[Dd] = I(g.dma_start(hb2[Dd % 2][:], src), "sG", 16)
                W(g, "G", ("sG", None))
            G(_)

            def _(t, Dd=Dd):
                W(t, "T", ("sG", m_hload[Dd]))
                if Dd >= 1:
                    W(t, "T", ("sV", m_mevac[Dd - 1]))
                m_mdone[Dd] = I(t.matmul(psM[:], jx[:], hb2[Dd % 2][:], start=True, stop=True), "sT")
            T(_)

            def _(v, Dd=Dd):
                W(v, "V", ("sT", m_mdone[Dd]))
                m_mevac[Dd] = I(v.tensor_copy(Mtiles[:, 128 * Dd:128 * Dd + 128], psM[:]), "sV")
            V(_)

        m_Mtiles = cnt["sV"]
        kst.close()
        kbar = dict(cnt)

        if "A" in phases:
            ast_ = contextlib.ExitStack()
            asb = lambda n, sh, dt: ast_.enter_context(nc.sbuf_tensor(n, sh, dt))
            aps = lambda n, sh, dt: ast_.enter_context(nc.psum_tensor(n, sh, dt))
            u_pan = asb("u_pan", [128, NT * PAN], F32)
            x1_pan = asb("x1_pan", [128, NT * PAN], F32)
            ulnb = asb("ulnb", [128, NT * PAN], BF16)
            xbf = [[asb(f"xbf{j}_{i}", [128, PAN], BF16) for i in range(2)] for j in range(2)]
            xsq = [[asb(f"xsq{j}_{i}", [128, PAN], BF16) for i in range(2)] for j in range(2)]
            x3t = [asb(f"x3t{i}", [128, PAN], F32) for i in range(2)]
            tfu = asb("tfu", [128, PAN], F32)
            tf2 = asb("tf2", [128, PAN], F32)
            xl2 = [asb(f"xl2{i}", [128, PAN], BF16) for i in range(2)]
            rowb = [asb(f"rowb{j}", [1, 3 * PAN], F32) for j in range(2)]
            rbc = [asb(f"rbc{j}", [128, PAN], F32) for j in range(2)]
            mrbc = [asb(f"mrbc{j}", [128, PAN], F32) for j in range(2)]
            psC = [aps(f"psC{i}", [128, PAN], F32) for i in range(2)]
            psS = [[aps(f"psS{j}_{i}", [128, PAN], F32) for i in range(2)] for j in range(2)]
            psB1 = aps("psB1", [128, PAN], F32)
            psB2 = aps("psB2", [128, PAN], F32)

            blk = lambda buf, k: buf[:, PAN * k:PAN * (k + 1)]
            mk = {
                "e12": [0] * NT, "ep": [0] * NT, "norm1": [0] * NT,
                "prev_ep": [0] * NT,
                "cp1": [0] * NT, "cp2": [0] * NT,
                "s1mm": [0] * NT, "s2mm": [0] * NT,
                "s1bmm": [0] * NT, "s2bmm": [0] * NT,
                "n2": [0] * NT, "xlst": [0] * NT, "upl": [0] * NT, "x3l": [0] * NT,
                "convd": [0] * NT,
                "statfree": [kbar["sV"], kbar["sV"]],
                "rbcfree": [kbar["sV"], kbar["sP"]],
                "bcevac": [kbar["sV"], kbar["sV"]],
                "bcd1": kbar["sV"], "bcevac2_saved": kbar["sV"],
                "x1s": 0,
            }

            def rows_and_bcast(j, ps_pair, s2mark):
                def _(v):
                    W(v, "V", ("sT", s2mark))
                    I(v.tensor_scalar(rowb[j][:, 0:PAN], ps_pair[0][0:1, :], 1.0 / D, None, ALU.mult), "sV")
                    I(v.tensor_scalar(rowb[j][:, PAN:2 * PAN], ps_pair[1][0:1, :], 1.0 / D, EPS, ALU.mult, ALU.add), "sV")
                    W(v, "V", ("sV", None))
                    I(v.tensor_tensor(rowb[j][:, 2 * PAN:3 * PAN], rowb[j][:, 0:PAN], rowb[j][:, 0:PAN], ALU.mult), "sV")
                    W(v, "V", ("sV", None))
                    I(v.tensor_tensor(rowb[j][:, PAN:2 * PAN], rowb[j][:, PAN:2 * PAN], rowb[j][:, 2 * PAN:3 * PAN], ALU.subtract), "sV")
                V(_)
                mk["statfree"][j] = cnt["sV"]

                def _(sc):
                    W(sc, "A", ("sV", cnt["sV"]))
                    I(sc.activation(rowb[j][:, 2 * PAN:3 * PAN], rowb[j][:, PAN:2 * PAN], AF.Ln), "sA")
                    W(sc, "A", ("sA", None))
                    I(sc.activation(rowb[j][:, 2 * PAN:3 * PAN], rowb[j][:, 2 * PAN:3 * PAN], AF.Exp, scale=-0.5), "sA")
                A(_)
                m_r = cnt["sA"]

                def _(v, m_r=m_r):
                    W(v, "V", ("sA", m_r))
                    I(v.tensor_tensor(rowb[j][:, 0:PAN], rowb[j][:, 0:PAN], rowb[j][:, 2 * PAN:3 * PAN], ALU.mult), "sV")
                V(_)
                m_mr = cnt["sV"]

                def _(t, m_r=m_r, m_mr=m_mr):
                    W(t, "T", ("sA", m_r), ("sV", max(m_mr, mk["bcevac"][1 - j])))
                    I(t.matmul(psB1[:, :], ones[0:1, 0:1].to_broadcast([1, 128]),
                               rowb[j][0:1, 2 * PAN:3 * PAN], start=True, stop=True), "sT")
                    I(t.matmul(psB2[:, :], ones[0:1, 0:1].to_broadcast([1, 128]),
                               rowb[j][0:1, 0:PAN], start=True, stop=True), "sT")
                T(_)
                m_bcmm = cnt["sT"]

                def _(v, m_bcmm=m_bcmm, j=j):
                    W(v, "V", ("sT", m_bcmm))
                    if j == 0:
                        W(v, "V", ("sV", mk["rbcfree"][0]))
                    else:
                        W(v, "V", ("sP", mk["rbcfree"][1]))
                    I(v.tensor_copy(rbc[j][:], psB1[:]), "sV")
                    I(v.tensor_copy(mrbc[j][:], psB2[:]), "sV")
                V(_)
                mk["bcevac"][j] = cnt["sV"]
                return cnt["sV"]

            def _(y):
                for s_, v_ in kbar.items():
                    W(y, "Y", (s_, v_))
                src = AP(uT.tensor, 0, [[R, 128], [128 * R, NT], [1, PAN]])
                dst = AP(u_pan, 0, [[NT * PAN, 128], [PAN, NT], [1, PAN]])
                I(y.dma_start(dst, src), "sUp0", 16)
            Y(_)
            m_up0 = cnt["sUp0"]

            sA0 = cnt["sA"]
            for k in range(NT):
                mk["cp1"][k] = sA0 + 2 * k + 2
            sT0 = cnt["sT"]
            for k in range(NT):
                mk["s1mm"][k] = sT0 + 2 * k + 1
                mk["s2mm"][k] = sT0 + 2 * k + 2

            def _(sc):
                for k in range(NT):
                    W(sc, "A", ("sUp0", m_up0))
                    if k >= 2:
                        W(sc, "A", ("sT", mk["s2mm"][k - 2]))
                    I(sc.copy(xbf[0][k % 2][:], blk(u_pan, k)), "sA")
                    W(sc, "A", ("sA", None))
                    got = I(sc.activation(xsq[0][k % 2][:], xbf[0][k % 2][:], AF.Square), "sA")
                    assert got == mk["cp1"][k]
            A(_)

            def _(t):
                for k in range(NT):
                    W(t, "T", ("sA", mk["cp1"][k]))
                    if k == 0:
                        W(t, "T", ("sV", mk["statfree"][0]))
                    got = I(t.matmul(psS[0][0][0:1, :], onesbf[:], xbf[0][k % 2][:],
                                     start=(k == 0), stop=(k == NT - 1)), "sT")
                    assert got == mk["s1mm"][k]
                    got = I(t.matmul(psS[0][1][0:1, :], onesbf[:], xsq[0][k % 2][:],
                                     start=(k == 0), stop=(k == NT - 1)), "sT")
                    assert got == mk["s2mm"][k]
            T(_)
            mk["bcd1"] = rows_and_bcast(0, psS[0], mk["s2mm"][NT - 1])
            sq1_saved = list(mk["cp1"])

            for p in range(NPAN):
                has_n2 = p >= 1
                has_ln2 = p >= 1
                has_next = p + 1 < NPAN

                tbase = cnt["sT"]
                tcur = tbase
                for it in range(NT):
                    tcur += 1
                    mk["convd"][it] = tcur
                    if has_ln2 and it < 16:
                        for k in (2 * it, 2 * it + 1):
                            tcur += 1
                            mk["s1bmm"][k] = tcur
                            tcur += 1
                            mk["s2bmm"][k] = tcur
                    if has_next and it >= 16:
                        for k in (2 * (it - 16), 2 * (it - 16) + 1):
                            tcur += 1
                            mk["s1mm"][k] = tcur
                            tcur += 1
                            mk["s2mm"][k] = tcur

                if has_ln2:
                    sA0 = cnt["sA"]
                    for k in range(NT):
                        mk["cp2"][k] = sA0 + 2 * k + 2
                    def _(sc, p=p):
                        for k in range(NT):
                            W(sc, "A", ("sV", mk["prev_ep"][k]))
                            if k >= 2:
                                W(sc, "A", ("sT", mk["s2bmm"][k - 2]))
                            I(sc.copy(xbf[1][k % 2][:], blk(x1_pan, k)), "sA")
                            W(sc, "A", ("sA", None))
                            got = I(sc.activation(xsq[1][k % 2][:], xbf[1][k % 2][:], AF.Square), "sA")
                            assert got == mk["cp2"][k]
                    A(_)

                svA0 = cnt["sV"]
                c_ = svA0
                for k in range(NT):
                    c_ += 2
                    mk["norm1"][k] = c_
                    c_ += 1
                    mk["e12"][k] = c_
                    if k >= 2:
                        c_ += 1
                        mk["ep"][k - 2] = c_
                mk["ep"][NT - 2] = c_ + 1
                mk["ep"][NT - 1] = c_ + 2
                def _(v, p=p):
                    for k in range(NT):
                        W(v, "V", ("sV", mk["bcd1"]))
                        if p > 0:
                            W(v, "V", ("sT", prev_conv_mark))
                        I(v.tensor_tensor(tfu[:], blk(u_pan, k), rbc[0][:], ALU.mult), "sV")
                        W(v, "V", ("sV", None))
                        got = I(v.tensor_tensor(blk(ulnb, k), tfu[:], mrbc[0][:], ALU.subtract), "sV")
                        assert got == mk["norm1"][k]
                        W(v, "V", ("sV", None))
                        if p > 0:
                            W(v, "V", ("sX1", mk["x1s"]))
                        if has_ln2:
                            W(v, "V", ("sA", mk["cp2"][k]))
                        got = I(v.scalar_tensor_tensor(blk(x1_pan, k), blk(ulnb, k), dcol[:],
                                                       blk(u_pan, k), ALU.mult, ALU.add), "sV")
                        assert got == mk["e12"][k]
                        if k >= 2:
                            W(v, "V", ("sT", mk["convd"][k - 2]))
                            got = I(v.tensor_tensor(blk(x1_pan, k - 2), psC[(k - 2) % 2][:],
                                                    blk(x1_pan, k - 2), ALU.add), "sV")
                            assert got == mk["ep"][k - 2]
                    for it in (NT - 2, NT - 1):
                        W(v, "V", ("sT", mk["convd"][it]))
                        got = I(v.tensor_tensor(blk(x1_pan, it), psC[it % 2][:],
                                                blk(x1_pan, it), ALU.add), "sV")
                        assert got == mk["ep"][it]
                V(_)
                mk["rbcfree"][0] = mk["norm1"][NT - 1]

                if has_next:
                    upl_cur = [0] * NT
                    for k in range(NT):
                        upl_cur[k] = cnt[f"sUp{k % 2}"] + 16 * (k // 2 + 1)
                    def _(y, p=p):
                        for k in range(NT):
                            if k >= 2:
                                W(y, "Y", (f"sUp{k % 2}", upl_cur[k - 2]))
                            W(y, "Y", ("sV", mk["e12"][k]), ("sA", sq1_saved[k]))
                            src = AP(uT.tensor, 128 * k * R + PAN * (p + 1), [[R, 128], [1, PAN]])
                            got = I(y.dma_start(blk(u_pan, k), src), f"sUp{k % 2}", 16)
                            mk["upl"][k] = got
                            assert got == upl_cur[k]
                    Y(_)

                    sA0 = cnt["sA"]
                    cp1_new = [sA0 + 2 * k + 2 for k in range(NT)]
                    mk["cp1"] = cp1_new

                for it in range(NT):
                    def _(t, it=it, p=p):
                        W(t, "T", ("sV", mk["norm1"][it]))
                        if it >= 2:
                            W(t, "T", ("sV", mk["ep"][it - 2]))
                        mm = None
                        for jt in range(it + 1):
                            mm = t.matmul(psC[it % 2][:, :],
                                          Mtiles[:, 128 * (it - jt):128 * (it - jt) + 128],
                                          blk(ulnb, jt), start=(jt == 0), stop=(jt == it))
                        got = I(mm, "sT")
                        assert got == mk["convd"][it]
                        if has_ln2 and it < 16:
                            for k in (2 * it, 2 * it + 1):
                                W(t, "T", ("sA", mk["cp2"][k]))
                                if k == 0:
                                    W(t, "T", ("sV", mk["statfree"][1]))
                                got = I(t.matmul(psS[1][0][0:1, :], onesbf[:], xbf[1][k % 2][:],
                                                 start=(k == 0), stop=(k == NT - 1)), "sT")
                                assert got == mk["s1bmm"][k]
                                got = I(t.matmul(psS[1][1][0:1, :], onesbf[:], xsq[1][k % 2][:],
                                                 start=(k == 0), stop=(k == NT - 1)), "sT")
                                assert got == mk["s2bmm"][k]
                        if has_next and it >= 16:
                            for k in (2 * (it - 16), 2 * (it - 16) + 1):
                                W(t, "T", ("sA", mk["cp1"][k]))
                                if k == 0:
                                    W(t, "T", ("sV", mk["statfree"][0]))
                                got = I(t.matmul(psS[0][0][0:1, :], onesbf[:], xbf[0][k % 2][:],
                                                 start=(k == 0), stop=(k == NT - 1)), "sT")
                                assert got == mk["s1mm"][k]
                                got = I(t.matmul(psS[0][1][0:1, :], onesbf[:], xsq[0][k % 2][:],
                                                 start=(k == 0), stop=(k == NT - 1)), "sT")
                                assert got == mk["s2mm"][k]
                    T(_)
                prev_conv_mark = mk["convd"][NT - 1]
                mk["prev_ep"] = list(mk["ep"])

                if has_next:
                    def _(sc, p=p, cp1_new=cp1_new):
                        for k in range(NT):
                            W(sc, "A", (f"sUp{k % 2}", mk["upl"][k]))
                            if k >= 2:
                                W(sc, "A", ("sT", mk["s2mm"][k - 2]))
                            I(sc.copy(xbf[0][k % 2][:], blk(u_pan, k)), "sA")
                            W(sc, "A", ("sA", None))
                            got = I(sc.activation(xsq[0][k % 2][:], xbf[0][k % 2][:], AF.Square), "sA")
                            assert got == cp1_new[k], (got, cp1_new[k])
                    A(_)

                def _(y, p=p):
                    W(y, "Y", ("sV", mk["ep"][NT - 1]))
                    dst = AP(x1scr.tensor, PAN * p, [[R, 128], [128 * R, NT], [1, PAN]])
                    src = AP(x1_pan, 0, [[NT * PAN, 128], [PAN, NT], [1, PAN]])
                    I(y.dma_start(dst, src), "sX1", 16)
                Y(_)
                mk["x1s"] = cnt["sX1"]

                if has_ln2:
                    mk["bcevac2_saved"] = rows_and_bcast(1, psS[1], mk["s2bmm"][NT - 1])

                spG0 = cnt["sP"]
                for k in range(NT):
                    mk["n2"][k] = spG0 + 2 * k + 2
                if has_n2:
                    x3l_cur = [0] * NT
                    for k in range(NT):
                        x3l_cur[k] = cnt[f"sX3{k % 2}"] + 16 * (k // 2 + 1)
                    xlst_prev = list(mk["xlst"])
                    xlst_cur = [0] * NT
                    for k in range(NT):
                        xlst_cur[k] = cnt[f"sXl{k % 2}"] + 16 * (k // 2 + 1)

                    def _(y, p=p, xlst_cur=xlst_cur):
                        W(y, "Y", ("sX1", mk["x1s"]))
                        for k in range(NT):
                            if k >= 2:
                                W(y, "Y", ("sP", mk["n2"][k - 2] - 1))
                            src = AP(x1scr.tensor, 128 * k * R + PAN * (p - 1), [[R, 128], [1, PAN]])
                            got = I(y.dma_start(x3t[k % 2][:], src), f"sX3{k % 2}", 16)
                            mk["x3l"][k] = got
                            assert got == x3l_cur[k]
                            if k >= 2:
                                W(y, "Y", ("sP", mk["n2"][k - 2]))
                                dst = AP(x1lscr.tensor, 128 * (k - 2) * R + PAN * (p - 1), [[R, 128], [1, PAN]])
                                got = I(y.dma_start(dst, xl2[(k - 2) % 2][:]), f"sXl{(k - 2) % 2}", 16)
                                mk["xlst"][k - 2] = got
                                assert got == xlst_cur[k - 2]
                        for k in (NT - 2, NT - 1):
                            W(y, "Y", ("sP", mk["n2"][k]))
                            dst = AP(x1lscr.tensor, 128 * k * R + PAN * (p - 1), [[R, 128], [1, PAN]])
                            got = I(y.dma_start(dst, xl2[k % 2][:]), f"sXl{k % 2}", 16)
                            mk["xlst"][k] = got
                            assert got == xlst_cur[k]
                    Y(_)

                    def _(g, p=p):
                        for k in range(NT):
                            W(g, "G", ("sV", mk["bcevac2_saved"]), ("sP", None))
                            W(g, "G", (f"sX3{k % 2}", mk["x3l"][k]))
                            W(g, "G", (f"sXl{k % 2}", xlst_cur[k - 2] if k >= 2 else xlst_prev[NT - 2 + k]))
                            I(g.tensor_tensor(tf2[:], x3t[k % 2][:], rbc[1][:], ALU.mult), "sP")
                            W(g, "G", ("sP", None))
                            got = I(g.tensor_tensor(xl2[k % 2][:], tf2[:], mrbc[1][:], ALU.subtract), "sP")
                            assert got == mk["n2"][k]
                    G(_)
                    mk["rbcfree"][1] = mk["n2"][NT - 1]

                if has_next:
                    mk["bcd1"] = rows_and_bcast(0, psS[0], mk["s2mm"][NT - 1])
                    sq1_saved = list(mk["cp1"])


            def tail_norm2(pp):
                spG0 = cnt["sP"]
                for k in range(NT):
                    mk["n2"][k] = spG0 + 2 * k + 2
                x3l_cur = [cnt[f"sX3{k % 2}"] + 16 * (k // 2 + 1) for k in range(NT)]
                xlst_prev = list(mk["xlst"])
                xlst_cur = [cnt[f"sXl{k % 2}"] + 16 * (k // 2 + 1) for k in range(NT)]
                def _(y, pp=pp, xlst_cur=xlst_cur):
                    W(y, "Y", ("sX1", mk["x1s"]))
                    for k in range(NT):
                        if k >= 2:
                            W(y, "Y", ("sP", mk["n2"][k - 2] - 1))
                        src = AP(x1scr.tensor, 128 * k * R + PAN * pp, [[R, 128], [1, PAN]])
                        got = I(y.dma_start(x3t[k % 2][:], src), f"sX3{k % 2}", 16)
                        mk["x3l"][k] = got
                        assert got == x3l_cur[k]
                        if k >= 2:
                            W(y, "Y", ("sP", mk["n2"][k - 2]))
                            dst = AP(x1lscr.tensor, 128 * (k - 2) * R + PAN * pp, [[R, 128], [1, PAN]])
                            got = I(y.dma_start(dst, xl2[(k - 2) % 2][:]), f"sXl{(k - 2) % 2}", 16)
                            mk["xlst"][k - 2] = got
                            assert got == xlst_cur[k - 2]
                    for k in (NT - 2, NT - 1):
                        W(y, "Y", ("sP", mk["n2"][k]))
                        dst = AP(x1lscr.tensor, 128 * k * R + PAN * pp, [[R, 128], [1, PAN]])
                        got = I(y.dma_start(dst, xl2[k % 2][:]), f"sXl{k % 2}", 16)
                        mk["xlst"][k] = got
                        assert got == xlst_cur[k]
                Y(_)
                def _(g, pp=pp):
                    for k in range(NT):
                        W(g, "G", ("sV", mk["bcevac2_saved"]), ("sP", None))
                        W(g, "G", (f"sX3{k % 2}", mk["x3l"][k]))
                        W(g, "G", (f"sXl{k % 2}", xlst_cur[k - 2] if k >= 2 else xlst_prev[NT - 2 + k]))
                        I(g.tensor_tensor(tf2[:], x3t[k % 2][:], rbc[1][:], ALU.mult), "sP")
                        W(g, "G", ("sP", None))
                        got = I(g.tensor_tensor(xl2[k % 2][:], tf2[:], mrbc[1][:], ALU.subtract), "sP")
                        assert got == mk["n2"][k]
                G(_)
                mk["rbcfree"][1] = mk["n2"][NT - 1]

            sA0 = cnt["sA"]
            for k in range(NT):
                mk["cp2"][k] = sA0 + 2 * k + 2
            sT0 = cnt["sT"]
            for k in range(NT):
                mk["s1bmm"][k] = sT0 + 2 * k + 1
                mk["s2bmm"][k] = sT0 + 2 * k + 2
            def _(sc):
                for k in range(NT):
                    W(sc, "A", ("sV", mk["prev_ep"][k]))
                    if k >= 2:
                        W(sc, "A", ("sT", mk["s2bmm"][k - 2]))
                    I(sc.copy(xbf[1][k % 2][:], blk(x1_pan, k)), "sA")
                    W(sc, "A", ("sA", None))
                    got = I(sc.activation(xsq[1][k % 2][:], xbf[1][k % 2][:], AF.Square), "sA")
                    assert got == mk["cp2"][k]
            A(_)
            def _(t):
                for k in range(NT):
                    W(t, "T", ("sA", mk["cp2"][k]))
                    if k == 0:
                        W(t, "T", ("sV", mk["statfree"][1]))
                    got = I(t.matmul(psS[1][0][0:1, :], onesbf[:], xbf[1][k % 2][:],
                                     start=(k == 0), stop=(k == NT - 1)), "sT")
                    assert got == mk["s1bmm"][k]
                    got = I(t.matmul(psS[1][1][0:1, :], onesbf[:], xsq[1][k % 2][:],
                                     start=(k == 0), stop=(k == NT - 1)), "sT")
                    assert got == mk["s2bmm"][k]
            T(_)
            mk["bcevac2_saved"] = rows_and_bcast(1, psS[1], mk["s2bmm"][NT - 1])
            tail_norm2(NPAN - 1)

            ast_.close()
        mt_stack.close()
        abar = dict(cnt)

        m_dwst = [0] * NFT
        if "B" in phases:
            bst = contextlib.ExitStack()
            bsb = lambda n, sh, dt: bst.enter_context(nc.sbuf_tensor(n, sh, dt))
            bps = lambda n, sh, dt: bst.enter_context(nc.psum_tensor(n, sh, dt))
            xfull = bsb("xfull", [128, NT * R], BF16)
            wgb = [bsb(f"wgb{i}", [128, D], BF16) for i in range(2)]
            wub = [bsb(f"wub{i}", [128, D], BF16) for i in range(2)]
            wi32 = bsb("wi32", [128, D], I32)
            dwc = [bsb(f"dwc{i}", [128, D], BF16) for i in range(2)]
            gsb = bsb("gsb", [128, NFT], F32)
            usb = bsb("usb", [128, NFT], F32)
            sgs = [bsb(f"sgs{i}", [128, PAN], F32) for i in range(2)]
            hb = [bsb(f"hb{i}", [128, R], BF16) for i in range(2)]
            psG = [bps(f"psG{i}", [128, PAN], F32) for i in range(4)]
            psU = [bps(f"psU{i}", [128, PAN], F32) for i in range(4)]

            def _(y):
                for s_, v_ in abar.items():
                    W(y, "Y", (s_, v_))
                src = AP(x1lscr.tensor, 0, [[R, 128], [128 * R, NT], [1, R]])
                dst = AP(xfull, 0, [[NT * R, 128], [R, NT], [1, R]])
                I(y.dma_start(dst, src), "sY", 16)
                I(y.dma_start(gsb[:], gsp), "sY", 16)
                I(y.dma_start(usb[:], usp), "sY", 16)
            Y(_)
            m_xfull = cnt["sY"]

            m_gmm = {}
            m_umm = {}
            m_sg = {}
            m_su = {}
            m_h = {}
            m_hst = [0] * NFT
            m_castg = [0] * NFT
            m_castu = [0] * NFT
            m_castd = [0] * NFT

            def emit_chain(ftc):
                def _(y, ft=ftc):
                    if ftc == 0:
                        for s_, v_ in abar.items():
                            W(y, "Y", (s_, v_))
                    else:
                        W(y, "Y", ("sV", m_castd[ftc - 1]))
                    src_g = AP(gwl.tensor, ftc * 128 * D, [[D, 128], [1, D]])
                    I(y.dma_start(wi32[:], src_g), "sWi", 16)
                Y(_)
                m_ldg = cnt["sWi"]

                def _(v, ft=ftc, m_ldg=m_ldg):
                    W(v, "V", ("sWi", m_ldg))
                    if ftc >= 2:
                        W(v, "V", ("sT", m_gmm[(ftc - 2, 3)]))
                    m_castg[ftc] = I(v.tensor_copy(wgb[ftc % 2][:], wi32[:]), "sV")
                V(_)

                def _(y, ft=ftc):
                    W(y, "Y", ("sV", m_castg[ftc]))
                    src_u = AP(uwl.tensor, ftc * 128 * D, [[D, 128], [1, D]])
                    I(y.dma_start(wi32[:], src_u), "sWi", 16)
                Y(_)
                m_ldu = cnt["sWi"]

                def _(v, ft=ftc, m_ldu=m_ldu):
                    W(v, "V", ("sWi", m_ldu))
                    if ftc >= 2:
                        W(v, "V", ("sT", m_umm[(ftc - 2, 3)]))
                    m_castu[ftc] = I(v.tensor_copy(wub[ftc % 2][:], wi32[:]), "sV")
                V(_)

                def _(y, ft=ftc):
                    W(y, "Y", ("sV", m_castu[ftc]))
                    src_d = AP(dwT.tensor, 128 * ftc * D, [[D, 128], [1, D]])
                    I(y.dma_start(wi32[:], src_d), "sWi", 16)
                Y(_)
                m_ldd = cnt["sWi"]

                def _(v, ft=ftc, m_ldd=m_ldd):
                    W(v, "V", ("sWi", m_ldd))
                    if ftc >= 2:
                        W(v, "V", (f"sDw{ftc % 2}", m_dwst[ftc - 2]))
                    m_castd[ftc] = I(v.tensor_copy(dwc[ftc % 2][:], wi32[:]), "sV")
                V(_)

                def _(y, ft=ftc):
                    W(y, "Y", ("sV", m_castd[ftc]))
                    dst_d = AP(dwbf.tensor, 128 * ftc * D, [[D, 128], [1, D]])
                    m_dwst[ftc] = I(y.dma_start(dst_d, dwc[ftc % 2][:]), f"sDw{ftc % 2}", 16)
                Y(_)


            emit_chain(0)
            for ft in range(NFT):
                if ft + 1 < NFT:
                    emit_chain(ft + 1)
                sA0 = cnt["sA"]
                sV0 = cnt["sV"]
                for c in range(4):
                    m_sg[(ft, c)] = sA0 + c + 1
                    m_h[(ft, c)] = sV0 + c + 1
                    m_su[(ft, c)] = m_h[(ft, c)]

                def _(t, ft=ft):
                    W(t, "T", ("sY", m_xfull))
                    for pair in range(2):
                        c0, c1 = 2 * pair, 2 * pair + 1
                        W(t, "T", ("sV", m_castg[ft]), ("sV", m_castu[ft]))
                        if ft >= 1:
                            W(t, "T", ("sA", m_sg[(ft - 1, c1)]),
                              ("sV", m_h[(ft - 1, c1)]))
                        mmg = {}
                        for k in range(NT):
                            for c in (c0, c1):
                                mmg[c] = t.matmul(psG[c][:, :], wgb[ft % 2][:, 128 * k:128 * k + 128],
                                                  xfull[:, R * k + PAN * c:R * k + PAN * c + PAN],
                                                  start=(k == 0), stop=(k == NT - 1))
                            for c in (c0, c1):
                                mmu = t.matmul(psU[c][:, :], wub[ft % 2][:, 128 * k:128 * k + 128],
                                               xfull[:, R * k + PAN * c:R * k + PAN * c + PAN],
                                               start=(k == 0), stop=(k == NT - 1))
                                mmg[4 + c] = mmu
                        m_gmm[(ft, c0)] = I(mmg[c0], "sT")
                        m_gmm[(ft, c1)] = I(mmg[c1], "sT")
                        m_umm[(ft, c0)] = I(mmg[4 + c0], "sT")
                        m_umm[(ft, c1)] = I(mmg[4 + c1], "sT")
                T(_)

                def _(s, ft=ft):
                    for c in range(4):
                        W(s, "A", ("sT", m_gmm[(ft, c)]))
                        if c >= 2:
                            W(s, "A", ("sV", m_h[(ft, c - 2)]))
                        elif ft >= 1:
                            W(s, "A", ("sV", m_h[(ft - 1, c + 2)]))
                        got = I(s.activation(sgs[c % 2][:], psG[c][:], AF.Silu,
                                             scale=gsb[:, ft:ft + 1]), "sA")
                        assert got == m_sg[(ft, c)], (got, m_sg[(ft, c)])
                A(_)

                def _(v, ft=ft):
                    for c in range(4):
                        W(v, "V", ("sT", m_umm[(ft, c)]), ("sA", m_sg[(ft, c)]))
                        if c >= 1:
                            W(v, "V", ("sV", m_h[(ft, c - 1)]))
                        elif ft >= 1:
                            W(v, "V", ("sV", m_h[(ft - 1, 3)]))
                        if ft >= 2 and c == 0:
                            W(v, "V", (f"sH{ft % 2}", m_hst[ft - 2]))
                        got = I(v.scalar_tensor_tensor(hb[ft % 2][:, PAN * c:PAN * (c + 1)],
                                                       psU[c][:], usb[:, ft:ft + 1], sgs[c % 2][:],
                                                       ALU.mult, ALU.mult), "sV")
                        assert got == m_h[(ft, c)], (got, m_h[(ft, c)])
                V(_)

                def _(y, ft=ft):
                    W(y, "Y", ("sV", m_h[(ft, 3)]))
                    dst = AP(hscr.tensor, 128 * ft * R, [[R, 128], [1, R]])
                    m_hst[ft] = I(y.dma_start(dst, hb[ft % 2][:]), f"sH{ft % 2}", 16)
                Y(_)

            bst.close()
        bbar = dict(cnt)

        if "C" in phases:
            cst = contextlib.ExitStack()
            csb = lambda n, sh, dt: cst.enter_context(nc.sbuf_tensor(n, sh, dt))
            cps = lambda n, sh, dt: cst.enter_context(nc.psum_tensor(n, sh, dt))
            NH = 43
            hpan = [csb(f"hpan{i}", [128, NH * PAN], BF16) for i in range(3)]
            wdb = [csb(f"wdb{i}", [128, DFF], BF16) for i in range(2)]
            x1t = [csb(f"x1t{i}", [128, PAN], F32) for i in range(2)]
            ot = [csb(f"ot{i}", [128, PAN], F32) for i in range(2)]
            dsb = csb("dsb", [128, NT], F32)
            psD = [cps(f"psD{i}", [128, PAN], F32) for i in range(2)]

            def wd_src(gi):
                dm = gi % NT
                return AP(dwbf.tensor, 128 * dm, [[D, 128], [128 * D, NFT], [1, 128]])

            def _(y):
                for s_, v_ in bbar.items():
                    W(y, "Y", (s_, v_))
                I(y.dma_start(dsb[:], dsp), "sY", 16)
            Y(_)
            m_dsb = cnt["sY"]

            def hload(y, pc, half):
                bi = (2 * pc + half) % 3
                f0 = NH * half
                nf = NH
                src = AP(hscr.tensor, 128 * f0 * R + PAN * pc, [[R, 128], [128 * R, nf], [1, PAN]])
                dst = AP(hpan[bi], 0, [[NH * PAN, 128], [PAN, nf], [1, PAN]])
                return I(y.dma_start(dst, src), f"sHp{bi}", 16)

            m_wd = {}
            m_x1l = {}
            m_dmm = {}
            m_dsv = {}
            m_o = {}
            m_ost = {}
            m_hp = {}
            m_half = {}

            def _(sc):
                for s_, v_ in bbar.items():
                    W(sc, "A", (s_, v_))
                if "B" in phases:
                    W(sc, "A", ("sDw0", m_dwst[NFT - 2]), ("sDw1", m_dwst[NFT - 1]))
                I(sc.dma_start(wdb[0][:], wd_src(0)), "sWd0", 16)
                m_wd[0] = cnt["sWd0"]
                I(sc.dma_start(wdb[1][:], wd_src(1)), "sWd1", 16)
                m_wd[1] = cnt["sWd1"]
            A(_)

            for pc in range(NPAN):
                def _(y, pc=pc):
                    if "B" in phases:
                        W(y, "Y", ("sH0", m_hst[NFT - 2]), ("sH1", m_hst[NFT - 1]))
                    if pc == 0:
                        m_hp[(0, 0)] = hload(y, 0, 0)
                        m_hp[(0, 1)] = hload(y, 0, 1)
                        m_hp[(1, 0)] = hload(y, 1, 0)
                    else:
                        W(y, "Y", ("sT", m_half[pc - 1]))
                        m_hp[(pc, 1)] = hload(y, pc, 1)
                        if pc + 1 < NPAN:
                            W(y, "Y", ("sT", m_dmm[pc * NT - 1]))
                            m_hp[(pc + 1, 0)] = hload(y, pc + 1, 0)
                Y(_)

                for dm in range(NT):
                    gi = pc * NT + dm

                    def _(y, pc=pc, dm=dm, gi=gi):
                        if gi >= 2:
                            W(y, "Y", ("sV", m_o[gi - 2]), (f"sOt{dm % 2}", m_ost.get(gi - 2, 0)))
                        src_x = AP(x1scr.tensor, 128 * dm * R + PAN * pc, [[R, 128], [1, PAN]])
                        m_x1l[gi] = I(y.dma_start(x1t[dm % 2][:], src_x), f"sXt{dm % 2}", 16)
                    Y(_)

                    def _(t, pc=pc, dm=dm, gi=gi):
                        bA = (2 * pc) % 3
                        bB = (2 * pc + 1) % 3
                        W(t, "T", (f"sWd{dm % 2}", m_wd[gi]), (f"sHp{bA}", m_hp[(pc, 0)]))
                        if gi >= 2:
                            W(t, "T", ("sV", m_dsv[gi - 2]))
                        mm = None
                        for ftt in range(NFT):
                            if ftt == NH:
                                W(t, "T", (f"sHp{bB}", m_hp[(pc, 1)]))
                            buf = hpan[bA] if ftt < NH else hpan[bB]
                            fl = ftt if ftt < NH else ftt - NH
                            mm = t.matmul(psD[dm % 2][:, :], wdb[dm % 2][:, 128 * ftt:128 * ftt + 128],
                                          buf[:, PAN * fl:PAN * (fl + 1)],
                                          start=(ftt == 0), stop=(ftt == NFT - 1))
                            if dm == NT - 1 and ftt == NH - 1:
                                m_half[pc] = I(mm, "sT")
                        m_dmm[gi] = I(mm, "sT")
                    T(_)

                    def _(sc, pc=pc, dm=dm, gi=gi):
                        W(sc, "A", ("sT", m_dmm[gi]))
                        if gi + 2 < NPAN * NT:
                            I(sc.dma_start(wdb[gi % 2][:], wd_src(gi + 2)), f"sWd{gi % 2}", 16)
                            m_wd[gi + 2] = cnt[f"sWd{gi % 2}"]
                    A(_)

                    def _(v, pc=pc, dm=dm, gi=gi):
                        W(v, "V", ("sT", m_dmm[gi]), ("sY", m_dsb))
                        if gi >= 2:
                            W(v, "V", ("sV", m_o[gi - 2]), (f"sOt{dm % 2}", m_ost.get(gi - 2, 0)))
                        W(v, "V", (f"sXt{dm % 2}", m_x1l[gi]))
                        m_o[gi] = I(v.scalar_tensor_tensor(ot[dm % 2][:], psD[dm % 2][:],
                                                           dsb[:, dm:dm + 1], x1t[dm % 2][:],
                                                           ALU.mult, ALU.add), "sV")
                        m_dsv[gi] = m_o[gi]
                    V(_)

                    def _(y, pc=pc, dm=dm, gi=gi):
                        W(y, "Y", ("sV", m_o[gi]))
                        dst = AP(outT.tensor, 128 * dm * R + PAN * pc, [[R, 128], [1, PAN]])
                        m_ost[gi] = I(y.dma_start(dst, ot[dm % 2][:]), f"sOt{dm % 2}", 16)
                    Y(_)

            def _(y):
                W(y, "Y", ("sOt0", cnt["sOt0"]), ("sOt1", cnt["sOt1"]))
            Y(_)
            cst.close()

    return nc


def _prep_inputs(inputs):
    f32 = np.float32
    C = _host_constants()
    u = np.asarray(inputs["L0_attn_u"], f32).reshape(-1, D)
    gwl = np.ascontiguousarray(np.asarray(inputs["L0_ffn_gate_w"], np.int32)
                               .reshape(NFT, 128, NT, 128).transpose(0, 3, 2, 1).reshape(NFT, 128, D))
    uwl = np.ascontiguousarray(np.asarray(inputs["L0_ffn_up_w"], np.int32)
                               .reshape(NFT, 128, NT, 128).transpose(0, 3, 2, 1).reshape(NFT, 128, D))
    dwT = np.ascontiguousarray(np.asarray(inputs["L0_ffn_down_w"], np.int32).T)
    gsp = np.ascontiguousarray(np.asarray(inputs["L0_ffn_gate_s"], f32)[:, 0].reshape(NFT, 128).T)
    usp = np.ascontiguousarray(np.asarray(inputs["L0_ffn_up_s"], f32)[:, 0].reshape(NFT, 128).T)
    dsp = np.ascontiguousarray(np.asarray(inputs["L0_ffn_down_s"], f32)[:, 0].reshape(NT, 128).T)
    small64 = np.empty((64, 9), f32)
    small64[:, 0:2] = np.asarray(inputs["L0_attn_Lambda_ri"], f32)
    small64[:, 2:4] = np.asarray(inputs["L0_attn_p_ri"], f32)
    small64[:, 4:6] = np.asarray(inputs["L0_attn_q_ri"], f32)
    small64[:, 6:8] = np.asarray(inputs["L0_attn_Ct_ri"], f32).reshape(64, 2)
    small64[:, 8] = np.asarray(inputs["L0_attn_Bp"], f32)[:, 0]
    tab = C["tab"].copy()
    vc = np.asarray(inputs["L0_attn_Vc_ri"], f32)
    tab[:, 0:64] = np.ascontiguousarray(vc[:, :, 0].T)
    tab[:, 64:128] = np.ascontiguousarray(vc[:, :, 1].T)
    sdv = np.empty((1, 2), f32)
    sdv[0, 0] = np.asarray(inputs["L0_attn_D"], f32).reshape(-1)[0]
    sdv[0, 1] = np.asarray(inputs["L0_attn_log_step"], f32).reshape(-1)[0]
    shared = dict(gwl=gwl, uwl=uwl, dwT=dwT, gsp=gsp, usp=usp, dsp=dsp,
                  small64=small64, tab64=tab, cb=C["cb"], m128=C["m128"],
                  jx=C["jx"], sdv=sdv)
    in_maps = []
    for c in range(NCORE):
        m = dict(shared)
        m["uT"] = np.ascontiguousarray(u[R * c:R * (c + 1)].T)
        in_maps.append(m)
    return in_maps


def kernel(**inputs):
    in_maps = _prep_inputs(inputs)
    nc = build_program("KABC", dbg=False)
    res = run_bass_kernel_spmd(nc, in_maps, core_ids=list(range(NCORE)))
    parts = [np.asarray(r["outT"], np.float32).T for r in res.results]
    return np.concatenate(parts, axis=0).reshape(4, 4096, 4096).astype(np.float32)


# version 14
# speedup vs baseline: 1.0432x; 1.0432x over previous
import contextlib
import numpy as np
import ml_dtypes
import concourse.bass as bass
import concourse.mybir as mybir
from concourse.ap import AP
from concourse.bass_utils import run_bass_kernel_spmd

F32 = mybir.dt.float32
BF16 = mybir.dt.bfloat16
I32 = mybir.dt.int32
AF = mybir.ActivationFunctionType
ALU = mybir.AluOpType

NCORE = 8
R = 2048
D = 4096
DFF = 11008
NT = D // 128
NFT = DFF // 128
PAN = 512
NPAN = R // PAN
EPS = 1e-5

_CONST_CACHE = {}


def _host_constants():
    if _CONST_CACHE:
        return _CONST_CACHE
    L = D
    lvec = (np.arange(L, dtype=np.float32) / np.float32(L)).astype(np.float32)
    arg = (np.complex64(-2j * np.pi) * lvec.astype(np.complex64)).astype(np.complex64)
    Omega = np.exp(arg).astype(np.complex64)
    cc = (np.complex64(2) / (np.complex64(1) + Omega)).astype(np.complex64)
    c2 = (np.complex64(2) * ((np.complex64(1) - Omega) / (np.complex64(1) + Omega))).astype(np.complex64)
    f32 = np.float32
    cb = np.empty((64, 2 * L), f32)
    cb[:, :L] = c2.real.astype(f32)[None, :]
    cb[:, L:] = c2.imag.astype(f32)[None, :]
    W64 = np.exp(2j * np.pi * np.outer(np.arange(64), np.arange(64)) / 64.0)
    TW = np.exp(2j * np.pi * np.outer(np.arange(64), np.arange(64)) / 4096.0) / 4096.0
    tab = np.zeros((64, 448), f32)
    tab[:, 128:192] = W64.real.astype(f32)
    tab[:, 192:256] = W64.imag.astype(f32)
    tab[:, 256:320] = (-W64.imag).astype(f32)
    tab[:, 320:384] = TW.real.astype(f32)
    tab[:, 384:448] = TW.imag.astype(f32)
    m128 = np.empty((128, 192), f32)
    m128[:, 0:32] = cc.real.astype(f32).reshape(32, 128).T
    m128[:, 32:64] = cc.imag.astype(f32).reshape(32, 128).T
    m128[:, 64:] = np.eye(128, dtype=f32)
    jx = np.zeros((128, 128), ml_dtypes.bfloat16)
    jx[np.arange(128), 127 - np.arange(128)] = 1
    _CONST_CACHE.update(cb=cb, tab=tab, m128=m128, jx=jx)
    return _CONST_CACHE


def build_program(phases="KABC", dbg=False):
    nc = bass.Bass("TRN2", target_bir_lowering=False)

    uT = nc.dram_tensor("uT", [D, R], F32, kind="ExternalInput").ap()
    gwl = nc.dram_tensor("gwl", [NFT, 128, D], I32, kind="ExternalInput").ap()
    uwl = nc.dram_tensor("uwl", [NFT, 128, D], I32, kind="ExternalInput").ap()
    dwT = nc.dram_tensor("dwT", [DFF, D], I32, kind="ExternalInput").ap()
    gsp = nc.dram_tensor("gsp", [128, NFT], F32, kind="ExternalInput").ap()
    usp = nc.dram_tensor("usp", [128, NFT], F32, kind="ExternalInput").ap()
    dsp = nc.dram_tensor("dsp", [128, NT], F32, kind="ExternalInput").ap()
    small64_i = nc.dram_tensor("small64", [64, 9], F32, kind="ExternalInput").ap()
    tab64_i = nc.dram_tensor("tab64", [64, 448], F32, kind="ExternalInput").ap()
    cb_i = nc.dram_tensor("cb", [64, 2 * D], F32, kind="ExternalInput").ap()
    m128_i = nc.dram_tensor("m128", [128, 192], F32, kind="ExternalInput").ap()
    jx_i = nc.dram_tensor("jx", [128, 128], BF16, kind="ExternalInput").ap()
    sdv_i = nc.dram_tensor("sdv", [1, 2], F32, kind="ExternalInput").ap()

    outT = nc.dram_tensor("outT", [D, R], F32, kind="ExternalOutput").ap()
    scr_kind = "ExternalOutput" if dbg else "Internal"
    kpad = nc.dram_tensor("kpad", [4352], F32, kind=scr_kind).ap()
    arscr = nc.dram_tensor("arscr", [2, D], F32, kind=scr_kind).ap()
    x1scr = nc.dram_tensor("x1scr", [D, R], F32, kind=scr_kind).ap()
    x1lscr = nc.dram_tensor("x1lscr", [D, R], BF16, kind=scr_kind).ap()
    hscr = nc.dram_tensor("hscr", [DFF, R], BF16, kind=scr_kind).ap()
    dwbt = nc.dram_tensor("dwbt", [NT, 128, NFT, 128], BF16, kind=scr_kind).ap()

    st = contextlib.ExitStack()

    sem_names = ["sY", "sG", "sT", "sV", "sA", "sP", "sX1", "sHp0", "sHp1", "sHp2",
                 "sH0", "sH1", "sDw0", "sDw1", "sWi", "sXl0", "sXl1", "sUp0", "sUp1", "sX30", "sX31", "sWd0", "sWd1", "sXt0", "sXt1", "sOt0", "sOt1"]
    sems = {n: st.enter_context(nc.semaphore(n)) for n in sem_names}
    cnt = {n: 0 for n in sem_names}
    waited = {e: {} for e in ["T", "V", "A", "G", "Y"]}

    def I(instr, sem, amt=1):
        instr.then_inc(sems[sem], amt)
        cnt[sem] += amt
        return cnt[sem]

    def W(eng, ename, *pairs):
        for sem, val in pairs:
            if val is None:
                val = cnt[sem]
            if val > waited[ename].get(sem, -1):
                eng.wait_ge(sems[sem], val)
                waited[ename][sem] = val

    ones = st.enter_context(nc.sbuf_tensor("ones", [128, 1], F32))
    onesbf = st.enter_context(nc.sbuf_tensor("onesbf", [128, 1], BF16))
    dcol = st.enter_context(nc.sbuf_tensor("dcol", [128, 1], F32))

    mt_stack = contextlib.ExitStack()
    Mtiles = mt_stack.enter_context(nc.sbuf_tensor("Mtiles", [128, 32 * 128], BF16))

    with st, nc.Block() as block:
        def T(fn): block.tensor(fn)
        def V(fn): block.vector(fn)
        def A(fn): block.scalar(fn)
        def G(fn): block.gpsimd(fn)
        def Y(fn): block.sync(fn)

        kst = contextlib.ExitStack()
        ksb = lambda n, sh, dt: kst.enter_context(nc.sbuf_tensor(n, sh, dt))
        kps = lambda n, sh, dt: kst.enter_context(nc.psum_tensor(n, sh, dt))
        cb = ksb("cbs", [64, 2 * D], F32)
        small64 = ksb("small64s", [64, 9], F32)
        tab64 = ksb("tab64s", [64, 448], F32)
        m128 = ksb("m128s", [128, 192], F32)
        jx = ksb("jxs", [128, 128], BF16)
        sdv = ksb("sdvs", [1, 2], F32)
        grgi = ksb("grgi", [64, 2 * D], F32)
        msq = ksb("msq", [64, D], F32)
        tk1 = ksb("tk1", [64, D], F32)
        rr = ksb("rr", [64, D], F32)
        qp = ksb("qp", [64, D], F32)
        bri = ksb("bri", [64, 2], F32)
        v8 = ksb("v8", [64, 8], F32)
        vt1 = ksb("vt1", [64, 2], F32)
        sA_sb = ksb("sA_sb", [128, 512], F32)
        kk = ksb("kk", [128, 16 * 32], F32)
        invs = ksb("invs", [1, 2], F32)
        inv64 = ksb("inv64", [64, 1], F32)
        arT = ksb("arT", [32, 128], F32)
        ARs = ksb("ARs", [64, 128], F32)
        A1s = ksb("A1s", [64, 128], F32)
        ksmall = ksb("ksmall", [64, 64], F32)
        zrow = ksb("zrow", [1, 256], F32)
        hb2 = [ksb(f"hb2_{i}", [128, 128], BF16) for i in range(2)]
        psA = kps("psA", [128, 512], F32)
        psF1 = kps("psF1", [64, 64], F32)
        psF2 = kps("psF2", [64, 64], F32)
        psK = kps("psK", [64, 64], F32)
        psM = kps("psM", [128, 128], F32)
        psB64 = kps("psB64", [128, 2], F32)

        def _(y):
            I(y.dma_start(cb[:], cb_i), "sY", 16)
            I(y.dma_start(small64[:], small64_i), "sY", 16)
            I(y.dma_start(tab64[:], tab64_i), "sY", 16)
            I(y.dma_start(m128[:], m128_i), "sY", 16)
            I(y.dma_start(jx[:], jx_i), "sY", 16)
            I(y.dma_start(sdv[:], sdv_i), "sY", 16)
        Y(_)
        k_loads = cnt["sY"]

        def _(v):
            I(v.memset(ones[:], 1.0), "sV")
            I(v.memset(onesbf[:], 1.0), "sV")
            I(v.memset(zrow[:], 0.0), "sV")
        V(_)
        m_ones = cnt["sV"]

        def _(s):
            W(s, "A", ("sY", k_loads))
            I(s.activation(invs[0:1, 0:1], sdv[0:1, 1:2], AF.Exp), "sA")
        A(_)
        m_step = cnt["sA"]

        def _(v):
            W(v, "V", ("sA", m_step))
            I(v.reciprocal(invs[0:1, 1:2], invs[0:1, 0:1]), "sV")
        V(_)
        m_invs = cnt["sV"]

        def _(t):
            W(t, "T", ("sV", m_invs))
            I(t.matmul(psB64[0:64, 0:1], ones[0:1, 0:1].to_broadcast([1, 64]), invs[0:1, 1:2],
                       start=True, stop=True), "sT")
            I(t.matmul(psB64[0:128, 1:2], ones[0:1, 0:1].to_broadcast([1, 128]), sdv[0:1, 0:1],
                       start=True, stop=True), "sT")
        T(_)
        m_bc1 = cnt["sT"]

        def _(v):
            W(v, "V", ("sT", m_bc1))
            I(v.tensor_copy(inv64[:], psB64[0:64, 0:1]), "sV")
            I(v.tensor_copy(dcol[:], psB64[0:128, 1:2]), "sV")
        V(_)
        m_bc2 = cnt["sV"]

        def _(v):
            W(v, "V", ("sV", m_bc2))
            I(v.tensor_scalar(grgi[:, :], cb[:, :], inv64[:], None, ALU.mult), "sV")
            W(v, "V", ("sV", None))
            I(v.tensor_scalar(grgi[:, 0:D], grgi[:, 0:D], small64[:, 0:1], None, ALU.subtract), "sV")
            I(v.tensor_scalar(grgi[:, D:], grgi[:, D:], small64[:, 1:2], None, ALU.subtract), "sV")
            W(v, "V", ("sV", None))
            I(v.tensor_tensor(msq[:], grgi[:, 0:D], grgi[:, 0:D], ALU.mult), "sV")
            I(v.tensor_tensor(tk1[:], grgi[:, D:], grgi[:, D:], ALU.mult), "sV")
            W(v, "V", ("sV", None))
            I(v.tensor_tensor(msq[:], msq[:], tk1[:], ALU.add), "sV")
            W(v, "V", ("sV", None))
            I(v.reciprocal(tk1[:], msq[:]), "sV")
            W(v, "V", ("sV", None))
            I(v.tensor_tensor(rr[:], grgi[:, 0:D], tk1[:], ALU.mult), "sV")
            I(v.tensor_tensor(qp[:], grgi[:, D:], tk1[:], ALU.mult), "sV")
        V(_)
        m_R = cnt["sV"]

        def _(t):
            W(t, "T", ("sY", k_loads))
            I(t.matmul(psF1[0:64, 0:1], tab64[:, 0:64], small64[:, 8:9], start=True, stop=True), "sT")
            I(t.matmul(psF1[0:64, 1:2], tab64[:, 64:128], small64[:, 8:9], start=True, stop=True), "sT")
        T(_)
        m_B = cnt["sT"]

        def _(v):
            W(v, "V", ("sT", m_B))
            I(v.tensor_copy(bri[:], psF1[0:64, 0:2]), "sV")
            W(v, "V", ("sV", None))
            def conj_mul(acr, aci, bcr, bci, j):
                I(v.tensor_tensor(vt1[:, 0:1], acr, bcr, ALU.mult), "sV")
                I(v.tensor_tensor(vt1[:, 1:2], aci, bci, ALU.mult), "sV")
                W(v, "V", ("sV", None))
                I(v.tensor_tensor(v8[:, j:j + 1], vt1[:, 0:1], vt1[:, 1:2], ALU.add), "sV")
                W(v, "V", ("sV", None))
                I(v.tensor_tensor(vt1[:, 0:1], acr, bci, ALU.mult), "sV")
                I(v.tensor_tensor(vt1[:, 1:2], aci, bcr, ALU.mult), "sV")
                W(v, "V", ("sV", None))
                I(v.tensor_tensor(v8[:, j + 1:j + 2], vt1[:, 0:1], vt1[:, 1:2], ALU.subtract), "sV")
                W(v, "V", ("sV", None))
            ctr, cti = small64[:, 6:7], small64[:, 7:8]
            pr_, pi_ = small64[:, 2:3], small64[:, 3:4]
            qr_, qi_ = small64[:, 4:5], small64[:, 5:6]
            br_, bi_ = bri[:, 0:1], bri[:, 1:2]
            conj_mul(ctr, cti, br_, bi_, 0)
            conj_mul(ctr, cti, pr_, pi_, 2)
            conj_mul(qr_, qi_, br_, bi_, 4)
            conj_mul(qr_, qi_, pr_, pi_, 6)
        V(_)
        m_v8 = cnt["sV"]

        def _(t):
            W(t, "T", ("sV", max(m_R, m_v8)))
            mm = None
            for c in range(32):
                t.matmul(psA[:, 16 * c:16 * c + 8], rr[:, 128 * c:128 * c + 128], v8[:],
                         start=True, stop=True)
                mm = t.matmul(psA[:, 16 * c + 8:16 * c + 16], qp[:, 128 * c:128 * c + 128], v8[:],
                              start=True, stop=True)
            I(mm, "sT")
        T(_)
        m_cauchy = cnt["sT"]

        def _(v):
            W(v, "V", ("sT", m_cauchy))
            W(v, "V", ("sV", None))
            I(v.tensor_copy(sA_sb[:], psA[:]), "sV")
            W(v, "V", ("sV", None))
            ko = lambda j: sA_sb[:, j:512:16]
            kkv = lambda i: kk[:, 32 * i:32 * i + 32]
            for idx, (jp, jq) in enumerate([(0, 9), (2, 11), (4, 13), (6, 15)]):
                W(v, "V", ("sV", None))
                I(v.tensor_tensor(kkv(2 * idx), ko(jp), ko(jq), ALU.add), "sV")
                W(v, "V", ("sV", None))
                I(v.tensor_tensor(kkv(2 * idx + 1), ko(jp + 1), ko(jq - 1), ALU.subtract), "sV")
            W(v, "V", ("sV", None))
            W(v, "V", ("sV", None))
            I(v.tensor_scalar(kkv(8), kkv(6), 1.0, None, ALU.add), "sV")
            W(v, "V", ("sV", None))
            W(v, "V", ("sV", None))
            I(v.tensor_tensor(kkv(9), kkv(8), kkv(8), ALU.mult), "sV")
            W(v, "V", ("sV", None))
            I(v.tensor_tensor(kkv(12), kkv(7), kkv(7), ALU.mult), "sV")
            W(v, "V", ("sV", None))
            W(v, "V", ("sV", None))
            I(v.tensor_tensor(kkv(9), kkv(9), kkv(12), ALU.add), "sV")
            W(v, "V", ("sV", None))
            W(v, "V", ("sV", None))
            I(v.reciprocal(kkv(9), kkv(9)), "sV")
            W(v, "V", ("sV", None))
            W(v, "V", ("sV", None))
            I(v.tensor_tensor(kkv(12), kkv(4), kkv(8), ALU.mult), "sV")
            W(v, "V", ("sV", None))
            I(v.tensor_tensor(kkv(13), kkv(5), kkv(7), ALU.mult), "sV")
            W(v, "V", ("sV", None))
            W(v, "V", ("sV", None))
            I(v.tensor_tensor(kkv(10), kkv(12), kkv(13), ALU.add), "sV")
            W(v, "V", ("sV", None))
            I(v.tensor_tensor(kkv(12), kkv(5), kkv(8), ALU.mult), "sV")
            W(v, "V", ("sV", None))
            I(v.tensor_tensor(kkv(13), kkv(4), kkv(7), ALU.mult), "sV")
            W(v, "V", ("sV", None))
            W(v, "V", ("sV", None))
            I(v.tensor_tensor(kkv(11), kkv(12), kkv(13), ALU.subtract), "sV")
            W(v, "V", ("sV", None))
            W(v, "V", ("sV", None))
            I(v.tensor_tensor(kkv(10), kkv(10), kkv(9), ALU.mult), "sV")
            W(v, "V", ("sV", None))
            I(v.tensor_tensor(kkv(11), kkv(11), kkv(9), ALU.mult), "sV")
            W(v, "V", ("sV", None))
            W(v, "V", ("sV", None))
            I(v.tensor_tensor(kkv(12), kkv(2), kkv(10), ALU.mult), "sV")
            W(v, "V", ("sV", None))
            I(v.tensor_tensor(kkv(13), kkv(3), kkv(11), ALU.mult), "sV")
            W(v, "V", ("sV", None))
            W(v, "V", ("sV", None))
            I(v.tensor_tensor(kkv(12), kkv(12), kkv(13), ALU.subtract), "sV")
            W(v, "V", ("sV", None))
            W(v, "V", ("sV", None))
            I(v.tensor_tensor(kkv(14), kkv(0), kkv(12), ALU.subtract), "sV")
            W(v, "V", ("sV", None))
            I(v.tensor_tensor(kkv(12), kkv(2), kkv(11), ALU.mult), "sV")
            W(v, "V", ("sV", None))
            I(v.tensor_tensor(kkv(13), kkv(3), kkv(10), ALU.mult), "sV")
            W(v, "V", ("sV", None))
            W(v, "V", ("sV", None))
            I(v.tensor_tensor(kkv(12), kkv(12), kkv(13), ALU.add), "sV")
            W(v, "V", ("sV", None))
            W(v, "V", ("sV", None))
            I(v.tensor_tensor(kkv(15), kkv(1), kkv(12), ALU.subtract), "sV")
            W(v, "V", ("sV", None))
            W(v, "V", ("sV", None))
            I(v.tensor_tensor(kkv(0), m128[:, 0:32], kkv(14), ALU.mult), "sV")
            W(v, "V", ("sV", None))
            I(v.tensor_tensor(kkv(1), m128[:, 32:64], kkv(15), ALU.mult), "sV")
            W(v, "V", ("sV", None))
            I(v.tensor_tensor(kkv(2), m128[:, 0:32], kkv(15), ALU.mult), "sV")
            W(v, "V", ("sV", None))
            I(v.tensor_tensor(kkv(3), m128[:, 32:64], kkv(14), ALU.mult), "sV")
            W(v, "V", ("sV", None))
            W(v, "V", ("sV", None))
            I(v.tensor_tensor(kkv(4), kkv(0), kkv(1), ALU.subtract), "sV")
            W(v, "V", ("sV", None))
            I(v.tensor_tensor(kkv(5), kkv(2), kkv(3), ALU.add), "sV")
        V(_)
        m_ar = cnt["sV"]

        m_psM_free = 0
        for half in range(2):
            def _(t, half=half):
                W(t, "T", ("sV", m_ar))
                if half == 1:
                    W(t, "T", ("sV", m_psM_free))
                nonlocal_box[0] = I(t.transpose(psM[0:32, 0:128],
                                                kk[:, 32 * (4 + half):32 * (4 + half) + 32],
                                                m128[:, 64:192]), "sT")
            nonlocal_box = [0]
            T(_)
            m_tr = nonlocal_box[0]

            def _(v, half=half, m_tr=m_tr):
                W(v, "V", ("sT", m_tr))
                if half == 1:
                    W(v, "V", ("sY", None))
                I(v.tensor_copy(arT[:], psM[0:32, 0:128]), "sV")
            V(_)
            m_psM_free = cnt["sV"]
            m_arT = cnt["sV"]

            def _(y, half=half, m_arT=m_arT):
                W(y, "Y", ("sV", m_arT))
                I(y.dma_start(AP(arscr.tensor, half * D, [[128, 32], [1, 128]]), arT[:]), "sY", 16)
            Y(_)
            m_arst = cnt["sY"]

            def _(y, half=half, m_arst=m_arst):
                W(y, "Y", ("sY", m_arst))
                src = AP(arscr.tensor, half * D, [[64, 64], [1, 64]])
                I(y.dma_start(ARs[:, 64 * half:64 * half + 64], src), "sY", 16)
            Y(_)
        m_AR = cnt["sY"]

        w64r = tab64[:, 128:192]; w64i = tab64[:, 192:256]; w64in = tab64[:, 256:320]
        twr = tab64[:, 320:384]; twi = tab64[:, 384:448]

        def _(t):
            W(t, "T", ("sY", m_AR), ("sV", m_v8))
            t.matmul(psF1[:, :], ARs[:, 0:64], w64r, start=True, stop=False)
            I(t.matmul(psF1[:, :], ARs[:, 64:128], w64in, start=False, stop=True), "sT")
            t.matmul(psF2[:, :], ARs[:, 0:64], w64i, start=True, stop=False)
            I(t.matmul(psF2[:, :], ARs[:, 64:128], w64r, start=False, stop=True), "sT")
        T(_)
        m_st1 = cnt["sT"]

        def _(v):
            W(v, "V", ("sT", m_st1))
            W(v, "V", ("sV", None))
            I(v.tensor_copy(ksmall[:], psF1[:]), "sV")
            W(v, "V", ("sV", None))
            I(v.tensor_copy(ARs[:, 0:64], psF2[:]), "sV")
            W(v, "V", ("sV", None))
            W(v, "V", ("sV", None))
            I(v.tensor_tensor(A1s[:, 0:64], ksmall[:], twr, ALU.mult), "sV")
            W(v, "V", ("sV", None))
            I(v.tensor_tensor(A1s[:, 64:128], ksmall[:], twi, ALU.mult), "sV")
            W(v, "V", ("sV", None))
            I(v.tensor_tensor(ARs[:, 64:128], ARs[:, 0:64], twi, ALU.mult), "sV")
            W(v, "V", ("sV", None))
            W(v, "V", ("sV", None))
            I(v.tensor_tensor(A1s[:, 0:64], A1s[:, 0:64], ARs[:, 64:128], ALU.subtract), "sV")
            W(v, "V", ("sV", None))
            I(v.tensor_tensor(ARs[:, 64:128], ARs[:, 0:64], twr, ALU.mult), "sV")
            W(v, "V", ("sV", None))
            W(v, "V", ("sV", None))
            I(v.tensor_tensor(A1s[:, 64:128], A1s[:, 64:128], ARs[:, 64:128], ALU.add), "sV")
        V(_)
        m_tw = cnt["sV"]

        def _(t):
            W(t, "T", ("sV", m_tw))
            t.matmul(psK[:, :], w64r, A1s[:, 0:64], start=True, stop=False)
            I(t.matmul(psK[:, :], w64in, A1s[:, 64:128], start=False, stop=True), "sT")
        T(_)
        m_st2 = cnt["sT"]

        def _(v):
            W(v, "V", ("sT", m_st2))
            I(v.tensor_copy(ksmall[:], psK[:]), "sV")
        V(_)
        m_ksb = cnt["sV"]

        def _(y):
            W(y, "Y", ("sV", max(m_ones, m_ksb)))
            I(y.dma_start(AP(kpad.tensor, 0, [[127, 1], [1, 127]]), zrow[0:1, 0:127]), "sY", 16)
            I(y.dma_start(AP(kpad.tensor, 4223, [[129, 1], [1, 129]]), zrow[0:1, 0:129]), "sY", 16)
            I(y.dma_start(AP(kpad.tensor, 127, [[64, 64], [1, 64]]), ksmall[:]), "sY", 16)
        Y(_)
        m_kpad = cnt["sY"]

        m_hload = [0] * 32
        m_mdone = [0] * 32
        m_mevac = [0] * 32
        for Dd in range(32):
            def _(g, Dd=Dd):
                W(g, "G", ("sY", m_kpad))
                if Dd >= 2:
                    W(g, "G", ("sT", m_mdone[Dd - 2]))
                src = AP(kpad.tensor, 128 * Dd, [[1, 128], [1, 128]])
                m_hload[Dd] = I(g.dma_start(hb2[Dd % 2][:], src), "sG", 16)
                W(g, "G", ("sG", None))
            G(_)

            def _(t, Dd=Dd):
                W(t, "T", ("sG", m_hload[Dd]))
                if Dd >= 1:
                    W(t, "T", ("sV", m_mevac[Dd - 1]))
                m_mdone[Dd] = I(t.matmul(psM[:], jx[:], hb2[Dd % 2][:], start=True, stop=True), "sT")
            T(_)

            def _(v, Dd=Dd):
                W(v, "V", ("sT", m_mdone[Dd]))
                m_mevac[Dd] = I(v.tensor_copy(Mtiles[:, 128 * Dd:128 * Dd + 128], psM[:]), "sV")
            V(_)

        m_Mtiles = cnt["sV"]
        kst.close()
        kbar = dict(cnt)

        if "A" in phases:
            ast_ = contextlib.ExitStack()
            asb = lambda n, sh, dt: ast_.enter_context(nc.sbuf_tensor(n, sh, dt))
            aps = lambda n, sh, dt: ast_.enter_context(nc.psum_tensor(n, sh, dt))
            u_pan = asb("u_pan", [128, NT * PAN], F32)
            x1_pan = asb("x1_pan", [128, NT * PAN], F32)
            ulnb = asb("ulnb", [128, NT * PAN], BF16)
            xbf = [[asb(f"xbf{j}_{i}", [128, PAN], BF16) for i in range(2)] for j in range(2)]
            xsq = [[asb(f"xsq{j}_{i}", [128, PAN], BF16) for i in range(2)] for j in range(2)]
            x3t = [asb(f"x3t{i}", [128, PAN], F32) for i in range(2)]
            tfu = asb("tfu", [128, PAN], F32)
            tf2 = asb("tf2", [128, PAN], F32)
            xl2 = [asb(f"xl2{i}", [128, PAN], BF16) for i in range(2)]
            rowb = [asb(f"rowb{j}", [1, 3 * PAN], F32) for j in range(2)]
            rbc = [asb(f"rbc{j}", [128, PAN], F32) for j in range(2)]
            mrbc = [asb(f"mrbc{j}", [128, PAN], F32) for j in range(2)]
            psC = [aps(f"psC{i}", [128, PAN], F32) for i in range(2)]
            psS = [[aps(f"psS{j}_{i}", [128, PAN], F32) for i in range(2)] for j in range(2)]
            psB1 = aps("psB1", [128, PAN], F32)
            psB2 = aps("psB2", [128, PAN], F32)

            blk = lambda buf, k: buf[:, PAN * k:PAN * (k + 1)]
            mk = {
                "e12": [0] * NT, "ep": [0] * NT, "norm1": [0] * NT,
                "prev_ep": [0] * NT,
                "cp1": [0] * NT, "cp2": [0] * NT,
                "s1mm": [0] * NT, "s2mm": [0] * NT,
                "s1bmm": [0] * NT, "s2bmm": [0] * NT,
                "n2": [0] * NT, "xlst": [0] * NT, "upl": [0] * NT, "x3l": [0] * NT,
                "convd": [0] * NT,
                "statfree": [kbar["sV"], kbar["sV"]],
                "rbcfree": [kbar["sV"], kbar["sP"]],
                "bcevac": [kbar["sV"], kbar["sV"]],
                "bcd1": kbar["sV"], "bcevac2_saved": kbar["sV"],
                "x1s": 0,
            }

            def rows_and_bcast(j, ps_pair, s2mark):
                def _(v):
                    W(v, "V", ("sT", s2mark))
                    I(v.tensor_scalar(rowb[j][:, 0:PAN], ps_pair[0][0:1, :], 1.0 / D, None, ALU.mult), "sV")
                    I(v.tensor_scalar(rowb[j][:, PAN:2 * PAN], ps_pair[1][0:1, :], 1.0 / D, EPS, ALU.mult, ALU.add), "sV")
                    W(v, "V", ("sV", None))
                    I(v.tensor_tensor(rowb[j][:, 2 * PAN:3 * PAN], rowb[j][:, 0:PAN], rowb[j][:, 0:PAN], ALU.mult), "sV")
                    W(v, "V", ("sV", None))
                    I(v.tensor_tensor(rowb[j][:, PAN:2 * PAN], rowb[j][:, PAN:2 * PAN], rowb[j][:, 2 * PAN:3 * PAN], ALU.subtract), "sV")
                V(_)
                mk["statfree"][j] = cnt["sV"]

                def _(sc):
                    W(sc, "A", ("sV", cnt["sV"]))
                    I(sc.activation(rowb[j][:, 2 * PAN:3 * PAN], rowb[j][:, PAN:2 * PAN], AF.Ln), "sA")
                    W(sc, "A", ("sA", None))
                    I(sc.activation(rowb[j][:, 2 * PAN:3 * PAN], rowb[j][:, 2 * PAN:3 * PAN], AF.Exp, scale=-0.5), "sA")
                A(_)
                m_r = cnt["sA"]

                def _(v, m_r=m_r):
                    W(v, "V", ("sA", m_r))
                    I(v.tensor_tensor(rowb[j][:, 0:PAN], rowb[j][:, 0:PAN], rowb[j][:, 2 * PAN:3 * PAN], ALU.mult), "sV")
                V(_)
                m_mr = cnt["sV"]

                def _(t, m_r=m_r, m_mr=m_mr):
                    W(t, "T", ("sA", m_r), ("sV", max(m_mr, mk["bcevac"][1 - j])))
                    I(t.matmul(psB1[:, :], ones[0:1, 0:1].to_broadcast([1, 128]),
                               rowb[j][0:1, 2 * PAN:3 * PAN], start=True, stop=True), "sT")
                    I(t.matmul(psB2[:, :], ones[0:1, 0:1].to_broadcast([1, 128]),
                               rowb[j][0:1, 0:PAN], start=True, stop=True), "sT")
                T(_)
                m_bcmm = cnt["sT"]

                def _(v, m_bcmm=m_bcmm, j=j):
                    W(v, "V", ("sT", m_bcmm))
                    if j == 0:
                        W(v, "V", ("sV", mk["rbcfree"][0]))
                    else:
                        W(v, "V", ("sP", mk["rbcfree"][1]))
                    I(v.tensor_copy(rbc[j][:], psB1[:]), "sV")
                    I(v.tensor_copy(mrbc[j][:], psB2[:]), "sV")
                V(_)
                mk["bcevac"][j] = cnt["sV"]
                return cnt["sV"]

            def _(y):
                for s_, v_ in kbar.items():
                    W(y, "Y", (s_, v_))
                src = AP(uT.tensor, 0, [[R, 128], [128 * R, NT], [1, PAN]])
                dst = AP(u_pan, 0, [[NT * PAN, 128], [PAN, NT], [1, PAN]])
                I(y.dma_start(dst, src), "sUp0", 16)
            Y(_)
            m_up0 = cnt["sUp0"]

            sA0 = cnt["sA"]
            for k in range(NT):
                mk["cp1"][k] = sA0 + 2 * k + 2
            sT0 = cnt["sT"]
            for k in range(NT):
                mk["s1mm"][k] = sT0 + 2 * k + 1
                mk["s2mm"][k] = sT0 + 2 * k + 2

            def _(sc):
                for k in range(NT):
                    W(sc, "A", ("sUp0", m_up0))
                    if k >= 2:
                        W(sc, "A", ("sT", mk["s2mm"][k - 2]))
                    I(sc.copy(xbf[0][k % 2][:], blk(u_pan, k)), "sA")
                    W(sc, "A", ("sA", None))
                    got = I(sc.activation(xsq[0][k % 2][:], xbf[0][k % 2][:], AF.Square), "sA")
                    assert got == mk["cp1"][k]
            A(_)

            def _(t):
                for k in range(NT):
                    W(t, "T", ("sA", mk["cp1"][k]))
                    if k == 0:
                        W(t, "T", ("sV", mk["statfree"][0]))
                    got = I(t.matmul(psS[0][0][0:1, :], onesbf[:], xbf[0][k % 2][:],
                                     start=(k == 0), stop=(k == NT - 1)), "sT")
                    assert got == mk["s1mm"][k]
                    got = I(t.matmul(psS[0][1][0:1, :], onesbf[:], xsq[0][k % 2][:],
                                     start=(k == 0), stop=(k == NT - 1)), "sT")
                    assert got == mk["s2mm"][k]
            T(_)
            mk["bcd1"] = rows_and_bcast(0, psS[0], mk["s2mm"][NT - 1])
            sq1_saved = list(mk["cp1"])

            for p in range(NPAN):
                has_n2 = p >= 1
                has_ln2 = p >= 1
                has_next = p + 1 < NPAN

                tbase = cnt["sT"]
                tcur = tbase
                for it in range(NT):
                    tcur += 1
                    mk["convd"][it] = tcur
                    if has_ln2 and it < 16:
                        for k in (2 * it, 2 * it + 1):
                            tcur += 1
                            mk["s1bmm"][k] = tcur
                            tcur += 1
                            mk["s2bmm"][k] = tcur
                    if has_next and it >= 16:
                        for k in (2 * (it - 16), 2 * (it - 16) + 1):
                            tcur += 1
                            mk["s1mm"][k] = tcur
                            tcur += 1
                            mk["s2mm"][k] = tcur

                if has_ln2:
                    sA0 = cnt["sA"]
                    for k in range(NT):
                        mk["cp2"][k] = sA0 + 2 * k + 2
                    def _(sc, p=p):
                        for k in range(NT):
                            W(sc, "A", ("sV", mk["prev_ep"][k]))
                            if k >= 2:
                                W(sc, "A", ("sT", mk["s2bmm"][k - 2]))
                            I(sc.copy(xbf[1][k % 2][:], blk(x1_pan, k)), "sA")
                            W(sc, "A", ("sA", None))
                            got = I(sc.activation(xsq[1][k % 2][:], xbf[1][k % 2][:], AF.Square), "sA")
                            assert got == mk["cp2"][k]
                    A(_)

                svA0 = cnt["sV"]
                c_ = svA0
                for k in range(NT):
                    c_ += 2
                    mk["norm1"][k] = c_
                    c_ += 1
                    mk["e12"][k] = c_
                    if k >= 2:
                        c_ += 1
                        mk["ep"][k - 2] = c_
                mk["ep"][NT - 2] = c_ + 1
                mk["ep"][NT - 1] = c_ + 2
                def _(v, p=p):
                    for k in range(NT):
                        W(v, "V", ("sV", mk["bcd1"]))
                        if p > 0:
                            W(v, "V", ("sT", prev_conv_mark))
                        I(v.tensor_tensor(tfu[:], blk(u_pan, k), rbc[0][:], ALU.mult), "sV")
                        W(v, "V", ("sV", None))
                        got = I(v.tensor_tensor(blk(ulnb, k), tfu[:], mrbc[0][:], ALU.subtract), "sV")
                        assert got == mk["norm1"][k]
                        W(v, "V", ("sV", None))
                        if p > 0:
                            W(v, "V", ("sX1", mk["x1s"]))
                        if has_ln2:
                            W(v, "V", ("sA", mk["cp2"][k]))
                        got = I(v.scalar_tensor_tensor(blk(x1_pan, k), blk(ulnb, k), dcol[:],
                                                       blk(u_pan, k), ALU.mult, ALU.add), "sV")
                        assert got == mk["e12"][k]
                        if k >= 2:
                            W(v, "V", ("sT", mk["convd"][k - 2]))
                            got = I(v.tensor_tensor(blk(x1_pan, k - 2), psC[(k - 2) % 2][:],
                                                    blk(x1_pan, k - 2), ALU.add), "sV")
                            assert got == mk["ep"][k - 2]
                    for it in (NT - 2, NT - 1):
                        W(v, "V", ("sT", mk["convd"][it]))
                        got = I(v.tensor_tensor(blk(x1_pan, it), psC[it % 2][:],
                                                blk(x1_pan, it), ALU.add), "sV")
                        assert got == mk["ep"][it]
                V(_)
                mk["rbcfree"][0] = mk["norm1"][NT - 1]

                if has_next:
                    upl_cur = [0] * NT
                    for k in range(NT):
                        upl_cur[k] = cnt[f"sUp{k % 2}"] + 16 * (k // 2 + 1)
                    def _(y, p=p):
                        for k in range(NT):
                            if k >= 2:
                                W(y, "Y", (f"sUp{k % 2}", upl_cur[k - 2]))
                            W(y, "Y", ("sV", mk["e12"][k]), ("sA", sq1_saved[k]))
                            src = AP(uT.tensor, 128 * k * R + PAN * (p + 1), [[R, 128], [1, PAN]])
                            got = I(y.dma_start(blk(u_pan, k), src), f"sUp{k % 2}", 16)
                            mk["upl"][k] = got
                            assert got == upl_cur[k]
                    Y(_)

                    sA0 = cnt["sA"]
                    cp1_new = [sA0 + 2 * k + 2 for k in range(NT)]
                    mk["cp1"] = cp1_new

                for it in range(NT):
                    def _(t, it=it, p=p):
                        W(t, "T", ("sV", mk["norm1"][it]))
                        if it >= 2:
                            W(t, "T", ("sV", mk["ep"][it - 2]))
                        mm = None
                        for jt in range(it + 1):
                            mm = t.matmul(psC[it % 2][:, :],
                                          Mtiles[:, 128 * (it - jt):128 * (it - jt) + 128],
                                          blk(ulnb, jt), start=(jt == 0), stop=(jt == it))
                        got = I(mm, "sT")
                        assert got == mk["convd"][it]
                        if has_ln2 and it < 16:
                            for k in (2 * it, 2 * it + 1):
                                W(t, "T", ("sA", mk["cp2"][k]))
                                if k == 0:
                                    W(t, "T", ("sV", mk["statfree"][1]))
                                got = I(t.matmul(psS[1][0][0:1, :], onesbf[:], xbf[1][k % 2][:],
                                                 start=(k == 0), stop=(k == NT - 1)), "sT")
                                assert got == mk["s1bmm"][k]
                                got = I(t.matmul(psS[1][1][0:1, :], onesbf[:], xsq[1][k % 2][:],
                                                 start=(k == 0), stop=(k == NT - 1)), "sT")
                                assert got == mk["s2bmm"][k]
                        if has_next and it >= 16:
                            for k in (2 * (it - 16), 2 * (it - 16) + 1):
                                W(t, "T", ("sA", mk["cp1"][k]))
                                if k == 0:
                                    W(t, "T", ("sV", mk["statfree"][0]))
                                got = I(t.matmul(psS[0][0][0:1, :], onesbf[:], xbf[0][k % 2][:],
                                                 start=(k == 0), stop=(k == NT - 1)), "sT")
                                assert got == mk["s1mm"][k]
                                got = I(t.matmul(psS[0][1][0:1, :], onesbf[:], xsq[0][k % 2][:],
                                                 start=(k == 0), stop=(k == NT - 1)), "sT")
                                assert got == mk["s2mm"][k]
                    T(_)
                prev_conv_mark = mk["convd"][NT - 1]
                mk["prev_ep"] = list(mk["ep"])

                if has_next:
                    def _(sc, p=p, cp1_new=cp1_new):
                        for k in range(NT):
                            W(sc, "A", (f"sUp{k % 2}", mk["upl"][k]))
                            if k >= 2:
                                W(sc, "A", ("sT", mk["s2mm"][k - 2]))
                            I(sc.copy(xbf[0][k % 2][:], blk(u_pan, k)), "sA")
                            W(sc, "A", ("sA", None))
                            got = I(sc.activation(xsq[0][k % 2][:], xbf[0][k % 2][:], AF.Square), "sA")
                            assert got == cp1_new[k], (got, cp1_new[k])
                    A(_)

                def _(y, p=p):
                    W(y, "Y", ("sV", mk["ep"][NT - 1]))
                    dst = AP(x1scr.tensor, PAN * p, [[R, 128], [128 * R, NT], [1, PAN]])
                    src = AP(x1_pan, 0, [[NT * PAN, 128], [PAN, NT], [1, PAN]])
                    I(y.dma_start(dst, src), "sX1", 16)
                Y(_)
                mk["x1s"] = cnt["sX1"]

                if has_ln2:
                    mk["bcevac2_saved"] = rows_and_bcast(1, psS[1], mk["s2bmm"][NT - 1])

                spG0 = cnt["sP"]
                for k in range(NT):
                    mk["n2"][k] = spG0 + 2 * k + 2
                if has_n2:
                    x3l_cur = [0] * NT
                    for k in range(NT):
                        x3l_cur[k] = cnt[f"sX3{k % 2}"] + 16 * (k // 2 + 1)
                    xlst_prev = list(mk["xlst"])
                    xlst_cur = [0] * NT
                    for k in range(NT):
                        xlst_cur[k] = cnt[f"sXl{k % 2}"] + 16 * (k // 2 + 1)

                    def _(y, p=p, xlst_cur=xlst_cur):
                        W(y, "Y", ("sX1", mk["x1s"]))
                        for k in range(NT):
                            if k >= 2:
                                W(y, "Y", ("sP", mk["n2"][k - 2] - 1))
                            src = AP(x1scr.tensor, 128 * k * R + PAN * (p - 1), [[R, 128], [1, PAN]])
                            got = I(y.dma_start(x3t[k % 2][:], src), f"sX3{k % 2}", 16)
                            mk["x3l"][k] = got
                            assert got == x3l_cur[k]
                            if k >= 2:
                                W(y, "Y", ("sP", mk["n2"][k - 2]))
                                dst = AP(x1lscr.tensor, 128 * (k - 2) * R + PAN * (p - 1), [[R, 128], [1, PAN]])
                                got = I(y.dma_start(dst, xl2[(k - 2) % 2][:]), f"sXl{(k - 2) % 2}", 16)
                                mk["xlst"][k - 2] = got
                                assert got == xlst_cur[k - 2]
                        for k in (NT - 2, NT - 1):
                            W(y, "Y", ("sP", mk["n2"][k]))
                            dst = AP(x1lscr.tensor, 128 * k * R + PAN * (p - 1), [[R, 128], [1, PAN]])
                            got = I(y.dma_start(dst, xl2[k % 2][:]), f"sXl{k % 2}", 16)
                            mk["xlst"][k] = got
                            assert got == xlst_cur[k]
                    Y(_)

                    def _(g, p=p):
                        for k in range(NT):
                            W(g, "G", ("sV", mk["bcevac2_saved"]), ("sP", None))
                            W(g, "G", (f"sX3{k % 2}", mk["x3l"][k]))
                            W(g, "G", (f"sXl{k % 2}", xlst_cur[k - 2] if k >= 2 else xlst_prev[NT - 2 + k]))
                            I(g.tensor_tensor(tf2[:], x3t[k % 2][:], rbc[1][:], ALU.mult), "sP")
                            W(g, "G", ("sP", None))
                            got = I(g.tensor_tensor(xl2[k % 2][:], tf2[:], mrbc[1][:], ALU.subtract), "sP")
                            assert got == mk["n2"][k]
                    G(_)
                    mk["rbcfree"][1] = mk["n2"][NT - 1]

                if has_next:
                    mk["bcd1"] = rows_and_bcast(0, psS[0], mk["s2mm"][NT - 1])
                    sq1_saved = list(mk["cp1"])


            def tail_norm2(pp):
                spG0 = cnt["sP"]
                for k in range(NT):
                    mk["n2"][k] = spG0 + 2 * k + 2
                x3l_cur = [cnt[f"sX3{k % 2}"] + 16 * (k // 2 + 1) for k in range(NT)]
                xlst_prev = list(mk["xlst"])
                xlst_cur = [cnt[f"sXl{k % 2}"] + 16 * (k // 2 + 1) for k in range(NT)]
                def _(y, pp=pp, xlst_cur=xlst_cur):
                    W(y, "Y", ("sX1", mk["x1s"]))
                    for k in range(NT):
                        if k >= 2:
                            W(y, "Y", ("sP", mk["n2"][k - 2] - 1))
                        src = AP(x1scr.tensor, 128 * k * R + PAN * pp, [[R, 128], [1, PAN]])
                        got = I(y.dma_start(x3t[k % 2][:], src), f"sX3{k % 2}", 16)
                        mk["x3l"][k] = got
                        assert got == x3l_cur[k]
                        if k >= 2:
                            W(y, "Y", ("sP", mk["n2"][k - 2]))
                            dst = AP(x1lscr.tensor, 128 * (k - 2) * R + PAN * pp, [[R, 128], [1, PAN]])
                            got = I(y.dma_start(dst, xl2[(k - 2) % 2][:]), f"sXl{(k - 2) % 2}", 16)
                            mk["xlst"][k - 2] = got
                            assert got == xlst_cur[k - 2]
                    for k in (NT - 2, NT - 1):
                        W(y, "Y", ("sP", mk["n2"][k]))
                        dst = AP(x1lscr.tensor, 128 * k * R + PAN * pp, [[R, 128], [1, PAN]])
                        got = I(y.dma_start(dst, xl2[k % 2][:]), f"sXl{k % 2}", 16)
                        mk["xlst"][k] = got
                        assert got == xlst_cur[k]
                Y(_)
                def _(g, pp=pp):
                    for k in range(NT):
                        W(g, "G", ("sV", mk["bcevac2_saved"]), ("sP", None))
                        W(g, "G", (f"sX3{k % 2}", mk["x3l"][k]))
                        W(g, "G", (f"sXl{k % 2}", xlst_cur[k - 2] if k >= 2 else xlst_prev[NT - 2 + k]))
                        I(g.tensor_tensor(tf2[:], x3t[k % 2][:], rbc[1][:], ALU.mult), "sP")
                        W(g, "G", ("sP", None))
                        got = I(g.tensor_tensor(xl2[k % 2][:], tf2[:], mrbc[1][:], ALU.subtract), "sP")
                        assert got == mk["n2"][k]
                G(_)
                mk["rbcfree"][1] = mk["n2"][NT - 1]

            sA0 = cnt["sA"]
            for k in range(NT):
                mk["cp2"][k] = sA0 + 2 * k + 2
            sT0 = cnt["sT"]
            for k in range(NT):
                mk["s1bmm"][k] = sT0 + 2 * k + 1
                mk["s2bmm"][k] = sT0 + 2 * k + 2
            def _(sc):
                for k in range(NT):
                    W(sc, "A", ("sV", mk["prev_ep"][k]))
                    if k >= 2:
                        W(sc, "A", ("sT", mk["s2bmm"][k - 2]))
                    I(sc.copy(xbf[1][k % 2][:], blk(x1_pan, k)), "sA")
                    W(sc, "A", ("sA", None))
                    got = I(sc.activation(xsq[1][k % 2][:], xbf[1][k % 2][:], AF.Square), "sA")
                    assert got == mk["cp2"][k]
            A(_)
            def _(t):
                for k in range(NT):
                    W(t, "T", ("sA", mk["cp2"][k]))
                    if k == 0:
                        W(t, "T", ("sV", mk["statfree"][1]))
                    got = I(t.matmul(psS[1][0][0:1, :], onesbf[:], xbf[1][k % 2][:],
                                     start=(k == 0), stop=(k == NT - 1)), "sT")
                    assert got == mk["s1bmm"][k]
                    got = I(t.matmul(psS[1][1][0:1, :], onesbf[:], xsq[1][k % 2][:],
                                     start=(k == 0), stop=(k == NT - 1)), "sT")
                    assert got == mk["s2bmm"][k]
            T(_)
            mk["bcevac2_saved"] = rows_and_bcast(1, psS[1], mk["s2bmm"][NT - 1])
            tail_norm2(NPAN - 1)

            ast_.close()
        mt_stack.close()
        abar = dict(cnt)

        m_dwst = [0] * NFT
        if "B" in phases:
            bst = contextlib.ExitStack()
            bsb = lambda n, sh, dt: bst.enter_context(nc.sbuf_tensor(n, sh, dt))
            bps = lambda n, sh, dt: bst.enter_context(nc.psum_tensor(n, sh, dt))
            xfull = bsb("xfull", [128, NT * R], BF16)
            wgb = [bsb(f"wgb{i}", [128, D], BF16) for i in range(2)]
            wub = [bsb(f"wub{i}", [128, D], BF16) for i in range(2)]
            wi32 = bsb("wi32", [128, D], I32)
            dwc = [bsb(f"dwc{i}", [128, D], BF16) for i in range(2)]
            gsb = bsb("gsb", [128, NFT], F32)
            usb = bsb("usb", [128, NFT], F32)
            sgs = [bsb(f"sgs{i}", [128, PAN], F32) for i in range(2)]
            hb = [bsb(f"hb{i}", [128, R], BF16) for i in range(2)]
            psG = [bps(f"psG{i}", [128, PAN], F32) for i in range(4)]
            psU = [bps(f"psU{i}", [128, PAN], F32) for i in range(4)]

            def _(y):
                for s_, v_ in abar.items():
                    W(y, "Y", (s_, v_))
                src = AP(x1lscr.tensor, 0, [[R, 128], [128 * R, NT], [1, R]])
                dst = AP(xfull, 0, [[NT * R, 128], [R, NT], [1, R]])
                I(y.dma_start(dst, src), "sY", 16)
                I(y.dma_start(gsb[:], gsp), "sY", 16)
                I(y.dma_start(usb[:], usp), "sY", 16)
            Y(_)
            m_xfull = cnt["sY"]

            m_gmm = {}
            m_umm = {}
            m_sg = {}
            m_su = {}
            m_h = {}
            m_hst = [0] * NFT
            m_castg = [0] * NFT
            m_castu = [0] * NFT
            m_castd = [0] * NFT

            def emit_chain(ftc):
                def _(y, ft=ftc):
                    if ftc == 0:
                        for s_, v_ in abar.items():
                            W(y, "Y", (s_, v_))
                    else:
                        W(y, "Y", ("sV", m_castd[ftc - 1]))
                    src_g = AP(gwl.tensor, ftc * 128 * D, [[D, 128], [1, D]])
                    I(y.dma_start(wi32[:], src_g), "sWi", 16)
                Y(_)
                m_ldg = cnt["sWi"]

                def _(v, ft=ftc, m_ldg=m_ldg):
                    W(v, "V", ("sWi", m_ldg))
                    if ftc >= 2:
                        W(v, "V", ("sT", m_gmm[(ftc - 2, 3)]))
                    m_castg[ftc] = I(v.tensor_copy(wgb[ftc % 2][:], wi32[:]), "sV")
                V(_)

                def _(y, ft=ftc):
                    W(y, "Y", ("sV", m_castg[ftc]))
                    src_u = AP(uwl.tensor, ftc * 128 * D, [[D, 128], [1, D]])
                    I(y.dma_start(wi32[:], src_u), "sWi", 16)
                Y(_)
                m_ldu = cnt["sWi"]

                def _(v, ft=ftc, m_ldu=m_ldu):
                    W(v, "V", ("sWi", m_ldu))
                    if ftc >= 2:
                        W(v, "V", ("sT", m_umm[(ftc - 2, 3)]))
                    m_castu[ftc] = I(v.tensor_copy(wub[ftc % 2][:], wi32[:]), "sV")
                V(_)

                def _(y, ft=ftc):
                    W(y, "Y", ("sV", m_castu[ftc]))
                    src_d = AP(dwT.tensor, 128 * ftc * D, [[D, 128], [1, D]])
                    I(y.dma_start(wi32[:], src_d), "sWi", 16)
                Y(_)
                m_ldd = cnt["sWi"]

                def _(v, ft=ftc, m_ldd=m_ldd):
                    W(v, "V", ("sWi", m_ldd))
                    if ftc >= 2:
                        W(v, "V", (f"sDw{ftc % 2}", m_dwst[ftc - 2]))
                    m_castd[ftc] = I(v.tensor_copy(dwc[ftc % 2][:], wi32[:]), "sV")
                V(_)

                def _(sc, ft=ftc):
                    W(sc, "A", ("sV", m_castd[ftc]))
                    dst_d = AP(dwbt.tensor, ftc * 128,
                               [[NFT * 128, 128], [128 * NFT * 128, NT], [1, 128]])
                    src_d = AP(dwc[ftc % 2], 0, [[D, 128], [128, NT], [1, 128]])
                    m_dwst[ftc] = I(sc.dma_start(dst_d, src_d), f"sDw{ftc % 2}", 16)
                A(_)


            emit_chain(0)
            for ft in range(NFT):
                if ft + 1 < NFT:
                    emit_chain(ft + 1)
                sA0 = cnt["sA"]
                sV0 = cnt["sV"]
                for c in range(4):
                    m_sg[(ft, c)] = sA0 + c + 1
                    m_h[(ft, c)] = sV0 + c + 1
                    m_su[(ft, c)] = m_h[(ft, c)]

                def _(t, ft=ft):
                    W(t, "T", ("sY", m_xfull))
                    for pair in range(2):
                        c0, c1 = 2 * pair, 2 * pair + 1
                        W(t, "T", ("sV", m_castg[ft]), ("sV", m_castu[ft]))
                        if ft >= 1:
                            W(t, "T", ("sA", m_sg[(ft - 1, c1)]),
                              ("sV", m_h[(ft - 1, c1)]))
                        mmg = {}
                        for k in range(NT):
                            for c in (c0, c1):
                                mmg[c] = t.matmul(psG[c][:, :], wgb[ft % 2][:, 128 * k:128 * k + 128],
                                                  xfull[:, R * k + PAN * c:R * k + PAN * c + PAN],
                                                  start=(k == 0), stop=(k == NT - 1))
                            for c in (c0, c1):
                                mmu = t.matmul(psU[c][:, :], wub[ft % 2][:, 128 * k:128 * k + 128],
                                               xfull[:, R * k + PAN * c:R * k + PAN * c + PAN],
                                               start=(k == 0), stop=(k == NT - 1))
                                mmg[4 + c] = mmu
                        m_gmm[(ft, c0)] = I(mmg[c0], "sT")
                        m_gmm[(ft, c1)] = I(mmg[c1], "sT")
                        m_umm[(ft, c0)] = I(mmg[4 + c0], "sT")
                        m_umm[(ft, c1)] = I(mmg[4 + c1], "sT")
                T(_)

                def _(s, ft=ft):
                    for c in range(4):
                        W(s, "A", ("sT", m_gmm[(ft, c)]))
                        if c >= 2:
                            W(s, "A", ("sV", m_h[(ft, c - 2)]))
                        elif ft >= 1:
                            W(s, "A", ("sV", m_h[(ft - 1, c + 2)]))
                        got = I(s.activation(sgs[c % 2][:], psG[c][:], AF.Silu,
                                             scale=gsb[:, ft:ft + 1]), "sA")
                        assert got == m_sg[(ft, c)], (got, m_sg[(ft, c)])
                A(_)

                def _(v, ft=ft):
                    for c in range(4):
                        W(v, "V", ("sT", m_umm[(ft, c)]), ("sA", m_sg[(ft, c)]))
                        if c >= 1:
                            W(v, "V", ("sV", m_h[(ft, c - 1)]))
                        elif ft >= 1:
                            W(v, "V", ("sV", m_h[(ft - 1, 3)]))
                        if ft >= 2 and c == 0:
                            W(v, "V", (f"sH{ft % 2}", m_hst[ft - 2]))
                        got = I(v.scalar_tensor_tensor(hb[ft % 2][:, PAN * c:PAN * (c + 1)],
                                                       psU[c][:], usb[:, ft:ft + 1], sgs[c % 2][:],
                                                       ALU.mult, ALU.mult), "sV")
                        assert got == m_h[(ft, c)], (got, m_h[(ft, c)])
                V(_)

                def _(y, ft=ft):
                    W(y, "Y", ("sV", m_h[(ft, 3)]))
                    dst = AP(hscr.tensor, 128 * ft * R, [[R, 128], [1, R]])
                    m_hst[ft] = I(y.dma_start(dst, hb[ft % 2][:]), f"sH{ft % 2}", 16)
                Y(_)

            bst.close()
        bbar = dict(cnt)

        if "C" in phases:
            cst = contextlib.ExitStack()
            csb = lambda n, sh, dt: cst.enter_context(nc.sbuf_tensor(n, sh, dt))
            cps = lambda n, sh, dt: cst.enter_context(nc.psum_tensor(n, sh, dt))
            NH = 43
            hpan = [csb(f"hpan{i}", [128, NH * PAN], BF16) for i in range(3)]
            wdb = [csb(f"wdb{i}", [128, DFF], BF16) for i in range(2)]
            x1t = [csb(f"x1t{i}", [128, PAN], F32) for i in range(2)]
            ot = [csb(f"ot{i}", [128, PAN], F32) for i in range(2)]
            dsb = csb("dsb", [128, NT], F32)
            psD = [cps(f"psD{i}", [128, PAN], F32) for i in range(2)]

            def wd_src(gi):
                dm = gi % NT
                return AP(dwbt.tensor, dm * 128 * NFT * 128, [[NFT * 128, 128], [1, NFT * 128]])

            def _(y):
                for s_, v_ in bbar.items():
                    W(y, "Y", (s_, v_))
                I(y.dma_start(dsb[:], dsp), "sY", 16)
            Y(_)
            m_dsb = cnt["sY"]

            def hload(y, pc, half):
                bi = (2 * pc + half) % 3
                f0 = NH * half
                nf = NH
                src = AP(hscr.tensor, 128 * f0 * R + PAN * pc, [[R, 128], [128 * R, nf], [1, PAN]])
                dst = AP(hpan[bi], 0, [[NH * PAN, 128], [PAN, nf], [1, PAN]])
                return I(y.dma_start(dst, src), f"sHp{bi}", 16)

            m_wd = {}
            m_x1l = {}
            m_dmm = {}
            m_dsv = {}
            m_o = {}
            m_ost = {}
            m_hp = {}
            m_half = {}

            def _(sc):
                for s_, v_ in bbar.items():
                    W(sc, "A", (s_, v_))
                if "B" in phases:
                    W(sc, "A", ("sDw0", m_dwst[NFT - 2]), ("sDw1", m_dwst[NFT - 1]))
                I(sc.dma_start(wdb[0][:], wd_src(0)), "sWd0", 16)
                m_wd[0] = cnt["sWd0"]
                I(sc.dma_start(wdb[1][:], wd_src(1)), "sWd1", 16)
                m_wd[1] = cnt["sWd1"]
            A(_)

            for pc in range(NPAN):
                def _(y, pc=pc):
                    if "B" in phases:
                        W(y, "Y", ("sH0", m_hst[NFT - 2]), ("sH1", m_hst[NFT - 1]))
                    if pc == 0:
                        m_hp[(0, 0)] = hload(y, 0, 0)
                        m_hp[(0, 1)] = hload(y, 0, 1)
                        m_hp[(1, 0)] = hload(y, 1, 0)
                    else:
                        W(y, "Y", ("sT", m_half[pc - 1]))
                        m_hp[(pc, 1)] = hload(y, pc, 1)
                        if pc + 1 < NPAN:
                            W(y, "Y", ("sT", m_dmm[pc * NT - 1]))
                            m_hp[(pc + 1, 0)] = hload(y, pc + 1, 0)
                Y(_)

                for dm in range(NT):
                    gi = pc * NT + dm

                    def _(y, pc=pc, dm=dm, gi=gi):
                        if gi >= 2:
                            W(y, "Y", ("sV", m_o[gi - 2]), (f"sOt{dm % 2}", m_ost.get(gi - 2, 0)))
                        src_x = AP(x1scr.tensor, 128 * dm * R + PAN * pc, [[R, 128], [1, PAN]])
                        m_x1l[gi] = I(y.dma_start(x1t[dm % 2][:], src_x), f"sXt{dm % 2}", 16)
                    Y(_)

                    def _(t, pc=pc, dm=dm, gi=gi):
                        bA = (2 * pc) % 3
                        bB = (2 * pc + 1) % 3
                        W(t, "T", (f"sWd{dm % 2}", m_wd[gi]), (f"sHp{bA}", m_hp[(pc, 0)]))
                        if gi >= 2:
                            W(t, "T", ("sV", m_dsv[gi - 2]))
                        mm = None
                        for ftt in range(NFT):
                            if ftt == NH:
                                W(t, "T", (f"sHp{bB}", m_hp[(pc, 1)]))
                            buf = hpan[bA] if ftt < NH else hpan[bB]
                            fl = ftt if ftt < NH else ftt - NH
                            mm = t.matmul(psD[dm % 2][:, :], wdb[dm % 2][:, 128 * ftt:128 * ftt + 128],
                                          buf[:, PAN * fl:PAN * (fl + 1)],
                                          start=(ftt == 0), stop=(ftt == NFT - 1))
                            if dm == NT - 1 and ftt == NH - 1:
                                m_half[pc] = I(mm, "sT")
                        m_dmm[gi] = I(mm, "sT")
                    T(_)

                    def _(sc, pc=pc, dm=dm, gi=gi):
                        W(sc, "A", ("sT", m_dmm[gi]))
                        if gi + 2 < NPAN * NT:
                            I(sc.dma_start(wdb[gi % 2][:], wd_src(gi + 2)), f"sWd{gi % 2}", 16)
                            m_wd[gi + 2] = cnt[f"sWd{gi % 2}"]
                    A(_)

                    def _(v, pc=pc, dm=dm, gi=gi):
                        W(v, "V", ("sT", m_dmm[gi]), ("sY", m_dsb))
                        if gi >= 2:
                            W(v, "V", ("sV", m_o[gi - 2]), (f"sOt{dm % 2}", m_ost.get(gi - 2, 0)))
                        W(v, "V", (f"sXt{dm % 2}", m_x1l[gi]))
                        m_o[gi] = I(v.scalar_tensor_tensor(ot[dm % 2][:], psD[dm % 2][:],
                                                           dsb[:, dm:dm + 1], x1t[dm % 2][:],
                                                           ALU.mult, ALU.add), "sV")
                        m_dsv[gi] = m_o[gi]
                    V(_)

                    def _(y, pc=pc, dm=dm, gi=gi):
                        W(y, "Y", ("sV", m_o[gi]))
                        dst = AP(outT.tensor, 128 * dm * R + PAN * pc, [[R, 128], [1, PAN]])
                        m_ost[gi] = I(y.dma_start(dst, ot[dm % 2][:]), f"sOt{dm % 2}", 16)
                    Y(_)

            def _(y):
                W(y, "Y", ("sOt0", cnt["sOt0"]), ("sOt1", cnt["sOt1"]))
            Y(_)
            cst.close()

    return nc


def _prep_inputs(inputs):
    f32 = np.float32
    C = _host_constants()
    u = np.asarray(inputs["L0_attn_u"], f32).reshape(-1, D)
    gwl = np.ascontiguousarray(np.asarray(inputs["L0_ffn_gate_w"], np.int32)
                               .reshape(NFT, 128, NT, 128).transpose(0, 3, 2, 1).reshape(NFT, 128, D))
    uwl = np.ascontiguousarray(np.asarray(inputs["L0_ffn_up_w"], np.int32)
                               .reshape(NFT, 128, NT, 128).transpose(0, 3, 2, 1).reshape(NFT, 128, D))
    dwT = np.ascontiguousarray(np.asarray(inputs["L0_ffn_down_w"], np.int32).T)
    gsp = np.ascontiguousarray(np.asarray(inputs["L0_ffn_gate_s"], f32)[:, 0].reshape(NFT, 128).T)
    usp = np.ascontiguousarray(np.asarray(inputs["L0_ffn_up_s"], f32)[:, 0].reshape(NFT, 128).T)
    dsp = np.ascontiguousarray(np.asarray(inputs["L0_ffn_down_s"], f32)[:, 0].reshape(NT, 128).T)
    small64 = np.empty((64, 9), f32)
    small64[:, 0:2] = np.asarray(inputs["L0_attn_Lambda_ri"], f32)
    small64[:, 2:4] = np.asarray(inputs["L0_attn_p_ri"], f32)
    small64[:, 4:6] = np.asarray(inputs["L0_attn_q_ri"], f32)
    small64[:, 6:8] = np.asarray(inputs["L0_attn_Ct_ri"], f32).reshape(64, 2)
    small64[:, 8] = np.asarray(inputs["L0_attn_Bp"], f32)[:, 0]
    tab = C["tab"].copy()
    vc = np.asarray(inputs["L0_attn_Vc_ri"], f32)
    tab[:, 0:64] = np.ascontiguousarray(vc[:, :, 0].T)
    tab[:, 64:128] = np.ascontiguousarray(vc[:, :, 1].T)
    sdv = np.empty((1, 2), f32)
    sdv[0, 0] = np.asarray(inputs["L0_attn_D"], f32).reshape(-1)[0]
    sdv[0, 1] = np.asarray(inputs["L0_attn_log_step"], f32).reshape(-1)[0]
    shared = dict(gwl=gwl, uwl=uwl, dwT=dwT, gsp=gsp, usp=usp, dsp=dsp,
                  small64=small64, tab64=tab, cb=C["cb"], m128=C["m128"],
                  jx=C["jx"], sdv=sdv)
    in_maps = []
    for c in range(NCORE):
        m = dict(shared)
        m["uT"] = np.ascontiguousarray(u[R * c:R * (c + 1)].T)
        in_maps.append(m)
    return in_maps


def kernel(**inputs):
    in_maps = _prep_inputs(inputs)
    nc = build_program("KABC", dbg=False)
    res = run_bass_kernel_spmd(nc, in_maps, core_ids=list(range(NCORE)))
    parts = [np.asarray(r["outT"], np.float32).T for r in res.results]
    return np.concatenate(parts, axis=0).reshape(4, 4096, 4096).astype(np.float32)
